# version 08
# baseline (speedup 1.0000x reference)
import os
import numpy as np
import ml_dtypes
import concourse.bass as bass
import concourse.mybir as mybir
from concourse.bass_utils import run_bass_kernel_spmd

F32 = mybir.dt.float32
BF16 = mybir.dt.bfloat16
AX = mybir.AxisListType.X
ALU = mybir.AluOpType
ACT = mybir.ActivationFunctionType

MAGIC = 12582912.0
D = 1024
H = 16
I = 2816
NIC = I // 128
QMAX = 127.0
LN_EPS = 1e-5
GN_EPS = 1e-5
NCORE = 8


def host_constants():
    gam = np.array([1.0 - 2.0 ** (-5.0 - i) for i in range(H)], dtype=np.float64)
    j = np.arange(128, dtype=np.float64)
    c = {}
    c["ident32"] = np.eye(128, dtype=np.float32)
    c["identbf"] = np.eye(128, dtype=ml_dtypes.bfloat16)
    c["ones1"] = np.ones((1, 128), dtype=np.float32)
    cz = (j[None, :] >= j[:, None]).astype(np.float32)
    c["causal2"] = np.concatenate([cz, cz], 1).astype(ml_dtypes.bfloat16)
    qs = np.zeros((128, 1024), np.float32)
    ks = np.zeros((128, 1024), np.float32)
    for oc in range(8):
        for half, h in ((0, 2 * oc), (64, 2 * oc + 1)):
            qs[half:half + 64, oc * 128:(oc + 1) * 128] = (gam[h] ** j / 8.0)[None, :]
            ks[half:half + 64, oc * 128:(oc + 1) * 128] = (gam[h] ** (-j))[None, :]
    c["qscale"] = qs
    c["kscale"] = ks
    gc = np.zeros((128, D), np.float32)
    for m in range(8):
        gc[0:64, 64 * m:64 * m + 64] = gam[2 * m] ** 128
        gc[64:128, 512 + 64 * m:512 + 64 * m + 64] = gam[2 * m + 1] ** 128
    c["gcscale"] = gc
    sel_e = np.zeros((8, 1024), np.float32)
    sel_o = np.zeros((8, 1024), np.float32)
    for oc in range(8):
        sel_e[oc, oc * 128:oc * 128 + 64] = 1.0
        sel_o[oc, oc * 128 + 64:oc * 128 + 128] = 1.0
    c["sel_e"] = sel_e
    c["sel_o"] = sel_o
    return c


class Sched:
    ENGS = ("SY", "SC", "VE", "GP", "TE")

    def __init__(self, nc, E):
        self.nc = nc
        self.E = E
        self.q = {e: [] for e in self.ENGS}
        self.sems = {}
        for e in self.ENGS:
            self.sems[e + "c"] = E(nc.semaphore("sem_" + e + "c"))
            self.sems[e + "x"] = E(nc.semaphore("sem_" + e + "x"))
            if e in ("SY", "SC", "GP"):
                self.sems[e + "d"] = E(nc.semaphore("sem_" + e + "d"))
        self.cnt = {k: 0 for k in self.sems}
        self.dsems = ["SYd", "SCd", "GPd"]
        self.ev = {}
        self._cur_eng = None
        self._cur_obj = None

    def bar(self, inst):
        e = self._cur_eng
        sem = self.sems[e + "x"]
        self.cnt[e + "x"] += 1
        inst.then_inc(sem, 1)
        self._cur_obj.wait_ge(sem, self.cnt[e + "x"])
        return inst

    def add(self, eng, fn, wait=(), tag=None, kind="c", dsem=None):
        for w in wait:
            if w is not None and w not in self.ev:
                raise KeyError(f"event {w!r} undefined (tag={tag} eng={eng})")
        if dsem is not None:
            assert kind == "d"
            if dsem not in self.sems:
                self.sems[dsem] = self.E(self.nc.semaphore("sem_" + dsem))
                self.cnt[dsem] = 0
                self.dsems.append(dsem)
            key = dsem
        else:
            key = eng + kind
        self.q[eng].append((tuple(w for w in wait if w is not None), fn, tag, kind, key))
        if kind == "d":
            self.cnt[key] += 16 * getattr(fn, "ndma", 1)
        else:
            self.cnt[key] += 1
        if tag is not None:
            self.ev[tag] = (self.sems[key], self.cnt[key])

    def alias(self, new, old):
        self.ev[new] = self.ev[old]

    def emit(self, block):
        secs = {"SY": block.sync, "SC": block.scalar, "VE": block.vector,
                "GP": block.gpsimd, "TE": block.tensor}
        for e in self.ENGS:
            items = self.q[e]
            if not items:
                continue

            def body(eng, e=e, items=items):
                self._cur_eng = e
                self._cur_obj = eng
                for waitl, fn, tag, kind, key in items:
                    best = {}
                    for w in waitl:
                        sem, val = self.ev[w]
                        kk = sem.name
                        if best.get(kk, (None, -1))[1] < val:
                            best[kk] = (sem, val)
                    for sem, val in best.values():
                        eng.wait_ge(sem, val)
                    sem = self.sems[key]
                    if kind == "d":
                        insts = fn(eng)
                        if not isinstance(insts, (list, tuple)):
                            insts = [insts]
                        n = getattr(fn, "ndma", 1)
                        assert len(insts) == n, f"{tag}: ndma {n} != {len(insts)}"
                        for inst in insts:
                            inst.then_inc(sem, 16)
                    else:
                        fn(eng).then_inc(sem, 1)
            secs[e](body)


def dma(n):
    def wrap(fn):
        fn.ndma = n
        return fn
    return wrap


def build_program(T):
    DBG = int(os.environ.get("KDBG", "99"))
    DUMP = int(os.environ.get("KDUMP", "0"))
    PRO = int(os.environ.get("KPRO", "99"))
    PA = int(os.environ.get("KPA", "99"))
    NG = T // 512
    NCH = T // 128
    NTT = T // 512
    nc = bass.Bass("TRN2", target_bir_lowering=False, debug=False,
                   detect_race_conditions=False)

    def din(name, shape, dt=F32):
        return nc.dram_tensor(name, shape, dt, kind="ExternalInput").ap()

    x_h = din("x", [T, D])
    wqT_h = din("wqT", [D, D]); wkT_h = din("wkT", [D, D])
    wvT_h = din("wvT", [D, D]); woT_h = din("woT", [D, D])
    gnw_h = din("gnw", [D]); gnb_h = din("gnb", [D])
    clipx_h = din("clipx", [1, 1]); clipd_h = din("clipd", [1, 1])
    wg_h = din("wg", [I, D]); wu_h = din("wu", [I, D]); wd_h = din("wd", [D, I])
    wgT_h = din("wgT", [D, I]); wuT_h = din("wuT", [D, I]); wdT_h = din("wdT", [I, D])
    cvec_h = din("cvec", [1, D])
    wmod_h = din("wmod", [D, 6 * D])
    CONSTS = host_constants()
    ch = {k: din(k, list(v.shape), BF16 if v.dtype == ml_dtypes.bfloat16 else F32)
          for k, v in CONSTS.items()}

    out_h = nc.dram_tensor("out", [T, D], F32, kind="ExternalOutput").ap()

    wbf_h = nc.dram_tensor("wbf", [4, D, D], BF16).ap()
    igst_h = nc.dram_tensor("igst", [NIC, 8, 128, 128], BF16).ap()
    iust_h = nc.dram_tensor("iust", [NIC, 8, 128, 128], BF16).ap()
    idst_h = nc.dram_tensor("idst", [NIC, 128, D], BF16).ap()
    h2sp_h = nc.dram_tensor("h2sp", [NCH, 128, D], F32).ap()
    x1sp_h = nc.dram_tensor("x1sp", [NCH, 128, D], F32).ap()
    tsp_h = nc.dram_tensor("tsp", [NCH, I, 128], F32).ap()
    rowb_h = nc.dram_tensor("rowb", [1, 2816], F32).ap()
    ccin_h = nc.dram_tensor("ccin", [1, 16], F32).ap()
    ccout_h = nc.dram_tensor("ccout", [1, 16], F32).ap()

    from contextlib import ExitStack
    with ExitStack() as stack:
        E = stack.enter_context
        sbt = lambda name, shape, dt=F32: E(nc.sbuf_tensor("sb_" + name, shape, dt))

        ident32 = sbt("ident32", [128, 128])
        identbf = sbt("identbf", [128, 128], BF16)
        ones1 = sbt("ones1", [1, 128])
        causal2 = sbt("causal2", [128, 256], BF16)
        qscale = sbt("qscale", [128, 1024])
        kscale = sbt("kscale", [128, 1024])
        gcscale = sbt("gcscale", [128, 1024])
        sel_e = sbt("sel_e", [8, 1024]); sel_o = sbt("sel_o", [8, 1024])
        gnwv = sbt("gnwv", [128, 8]); gnbv = sbt("gnbv", [128, 8])
        clipx = sbt("clipx", [1, 1]); clipd = sbt("clipd", [1, 1])
        cfm = sbt("cfm", [128, 8]); silc = sbt("silc", [128, 8], BF16)
        modfm = sbt("modfm", [128, 48])
        onca = sbt("onca", [128, 8]); sa_v = sbt("sa_v", [128, 8])
        oncb = sbt("oncb", [128, 8]); sbv = sbt("sbv", [128, 8])
        ga_v = sbt("ga_v", [128, 8]); gb_v = sbt("gb_v", [128, 8])
        ga_bc = sbt("ga_bc", [128, 1024])
        mg_fm = sbt("mg_fm", [128, NIC]); mu_fm = sbt("mu_fm", [128, NIC])
        md_fm = sbt("md_fm", [128, 8])
        lamg = sbt("lamg", [128, NIC]); lamu = sbt("lamu", [128, NIC])
        rx_vec = sbt("rx_vec", [128, 1]); rd_vec = sbt("rd_vec", [128, 1])
        scal = sbt("scal", [1, 16]); scal2 = sbt("scal2", [1, 16])
        stats = sbt("stats", [128, 64])
        stat2 = sbt("stat2", [128, 16])
        tmp8 = sbt("tmp8", [128, 16])
        gnsy = sbt("gnsy", [128, 8]); gnsyy = sbt("gnsyy", [128, 8])
        gnst = sbt("gnst", [128, 48])
        hse = sbt("hse", [8, 8]); hso = sbt("hso", [8, 8])
        gnA = sbt("gnA", [128, 8]); gnB = sbt("gnB", [128, 8])
        lamfm = sbt("lamfm", [128, 8])
        mxb = sbt("mxb", [128, NCH]); mxd = sbt("mxd", [128, NIC * NTT])
        col1 = sbt("col1", [128, 1])
        epsv = sbt("epsv", [128, 1])
        negmg = sbt("negmg", [128, 1])
        r22 = sbt("r22", [22, 128])
        S_all = [sbt("S_a", [128, 1024], BF16), sbt("S_b", [128, 1024], BF16)]
        ar3 = sbt("ar3", [128, 4096])
        ar3b = ar3.bitcast(BF16)
        xbuf = [ar3[:, 0:1024], ar3[:, 1024:2048], ar3[:, 2048:3072], ar3[:, 3072:4096]]
        lam_bc = ar3[:, 2816:3840]
        ihc = [ar3b[:, 0:2816], ar3b[:, 2816:5632]]
        zbuf = [sbt("zb0", [128, 1024], BF16), sbt("zb1", [128, 1024], BF16)]
        ktb = [sbt("ktb0", [128, 1024], BF16), sbt("ktb1", [128, 1024], BF16)]
        vtb2 = [sbt("vtb0", [128, 1024], BF16), sbt("vtb1", [128, 1024], BF16)]
        amb = [sbt("amb0", [128, 256], BF16), sbt("amb1", [128, 256], BF16)]
        ybig = sbt("ybig", [128, 8 * T], BF16)
        ar1 = sbt("ar1", [128, 11264])
        ar2 = sbt("ar2", [128, 9088])
        ar1b = ar1.bitcast(BF16)
        ar2b = ar2.bitcast(BF16)

        vtb = vtb2 + [ar1b[:, 12288:13312], ar1b[:, 13312:14336]]
        wtk = [ar1b[:, 14336:15360], ar1b[:, 15360:16384]]
        wtq = [ar1b[:, 20480:21504], ar1b[:, 21504:22528]]
        h2b = [ar1[:, 4096:5120], ar1[:, 5120:6144]]
        ynb = ar2b[:, 4096:5120]
        ynbs = [ar2b[:, 4096:5120], ar2b[:, 5120:6144]]
        qsb = lambda: ar1b[:, 0:4096]
        ksb = lambda: ar1b[:, 4096:8192]
        hbf = lambda: ar1b[:, 8192:12288]
        wkw = lambda: ar1b[:, 12288:16384]
        wvw = lambda: ar1b[:, 16384:20480]
        wmt = [lambda: ar1b[:, 0:8192], lambda: ar1b[:, 8192:16384]]
        igb = [lambda: ar1b[:, 0:2048], lambda: ar1b[:, 2048:4096]]
        idres = lambda: ar1b[:, 0:NIC * 1024]
        wna = [lambda: ar2[:, 0:1024], lambda: ar2[:, 1024:2048]]
        wqa = [lambda: ar2[:, 2048:3456], lambda: ar2[:, 3456:4864]]
        iqa = [lambda: ar2b[:, 9728:11136], lambda: ar2b[:, 11136:12544]]
        rbc = lambda: ar2[:, 6272:9088]
        x1b = [lambda: ar2[:, 0:1024], lambda: ar2[:, 1024:2048]]
        tst = [lambda: ar2[:, 2048:2560], lambda: ar2[:, 2560:3072]]
        sgl = [lambda: ar2[:, 3072:3584], lambda: ar2[:, 3584:4096]]
        tcc = [lambda: ar2[:, 4096:5504], lambda: ar2[:, 5504:6912]]
        outb = [lambda: ar2[:, 6912:7936], lambda: ar2[:, 7936:8960]]
        rrow = ar2[0:1, 2048:4864]
        garow = ar2[0:1, 6272:7296]
        lamrow = ar2[0:1, 7296:8320]

        ps0 = E(nc.psum_tensor("ps0", [128, 512], F32))
        ps1 = E(nc.psum_tensor("ps1", [128, 512], F32))
        ps2 = E(nc.psum_tensor("ps2", [128, 512], F32))
        ps3 = E(nc.psum_tensor("ps3", [128, 512], F32))
        ps4 = E(nc.psum_tensor("ps4", [128, 512], F32))
        ps5 = E(nc.psum_tensor("ps5", [128, 512], F32))
        psS = E(nc.psum_tensor("psS", [128, 1024], F32))
        ps0b = ps0.bitcast(BF16); ps1b = ps1.bitcast(BF16)
        ps2b = ps2.bitcast(BF16); ps3b = ps3.bitcast(BF16)

        S = Sched(nc, E)
        A = S.add
        pfree = {}

        @dma(14)
        def ld_consts(sy):
            r = []
            for st, hh in ((ident32, ch["ident32"]), (identbf, ch["identbf"]),
                           (ones1, ch["ones1"]), (causal2, ch["causal2"]),
                           (qscale, ch["qscale"]), (kscale, ch["kscale"]),
                           (gcscale, ch["gcscale"]),
                           (sel_e, ch["sel_e"]), (sel_o, ch["sel_o"]),
                           (clipx, clipx_h), (clipd, clipd_h)):
                r.append(sy.dma_start(out=st[:], in_=hh))
            r.append(sy.dma_start(out=gnwv[:], in_=gnw_h.rearrange("(j p) -> p j", p=128),
                                  allow_slow_non_contiguous=True))
            r.append(sy.dma_start(out=gnbv[:], in_=gnb_h.rearrange("(j p) -> p j", p=128),
                                  allow_slow_non_contiguous=True))
            r.append(sy.dma_start(out=cfm[:], in_=cvec_h.rearrange("u (j p) -> p u j", p=128),
                                  allow_slow_non_contiguous=True))
            return r
        A("SY", ld_consts, tag="ldc", kind="d")
        for k in ("ps0", "ps1", "ps2", "ps3", "ps4", "ps5", "psS"):
            pfree[k] = "ldc"

        def init0(v):
            v.memset(epsv[:], LN_EPS)
            v.memset(negmg[:], -MAGIC)
            v.memset(S_all[0][:], 0.0)
            v.memset(S_all[1][:], 0.0)
            v.memset(scal[:], 0.0)
            return v.memset(scal2[:], 0.0)
        A("VE", init0, wait=("ldc",), tag="init0")

        if PRO < 1:
            def _dfin(gp):
                for kk in tuple(S.dsems):
                    if S.cnt[kk]:
                        gp.wait_ge(S.sems[kk], S.cnt[kk])
                return gp.engine_nop()
            A("GP", _dfin)
            S.emit(E(nc.Block()))
            return nc

        @dma(4)
        def wstage(gp):
            return [gp.dma_start(out=wbf_h[k], in_=wh)
                    for k, wh in enumerate((wqT_h, wkT_h, wvT_h, woT_h))]
        A("GP", wstage, tag="wstage", kind="d", dsem="WST")

        if PRO < 2:
            def _dfin(gp):
                for kk in tuple(S.dsems):
                    if S.cnt[kk]:
                        gp.wait_ge(S.sems[kk], S.cnt[kk])
                return gp.engine_nop()
            A("GP", _dfin)
            S.emit(E(nc.Block()))
            return nc

        def silu_c(sc):
            return sc.activation(silc[:], cfm[:], ACT.Silu)
        A("SC", silu_c, wait=("ldc",), tag="silc")

        for b8 in range(6):
            bb = b8 % 2

            @dma(1)
            def wmod_ld(gp, b8=b8, bb=bb):
                return gp.dma_start(
                    out=wmt[bb]().rearrange("p (fc n) -> p fc n", fc=8),
                    in_=wmod_h[:, b8 * 1024:(b8 + 1) * 1024]
                    .rearrange("(fc p) n -> p fc n", p=128))
            A("GP", wmod_ld, wait=("ldc", "wstage") + ((f"modmm{b8-2}",) if b8 >= 2 else ()),
              tag=f"wmod{b8}", kind="d", dsem=f"WMD{bb}")

            def modmm(t, b8=b8, bb=bb):
                pst = ps4 if b8 % 2 == 0 else ps5
                for lb in range(8):
                    for fc in range(8):
                        inst = t.matmul(pst[:, lb:lb + 1],
                                        wmt[bb]()[:, fc * 1024 + lb * 128:fc * 1024 + lb * 128 + 128],
                                        silc[:, fc:fc + 1],
                                        start=(fc == 0), stop=(fc == 7))
                return inst
            pn = "ps4" if b8 % 2 == 0 else "ps5"
            A("TE", modmm, wait=(f"wmod{b8}", "silc", pfree[pn]), tag=f"modmm{b8}")
            pfree[pn] = f"modev{b8}"

            def modev(v, b8=b8):
                pst = ps4 if b8 % 2 == 0 else ps5
                return v.tensor_copy(modfm[:, b8 * 8:(b8 + 1) * 8], pst[:, 0:8])
            A("VE", modev, wait=(f"modmm{b8}",), tag=f"modev{b8}")

        if PRO < 3:
            def _dfin(gp):
                for kk in tuple(S.dsems):
                    if S.cnt[kk]:
                        gp.wait_ge(S.sems[kk], S.cnt[kk])
                return gp.engine_nop()
            A("GP", _dfin)
            S.emit(E(nc.Block()))
            return nc

        def modpost(v):
            S.bar(v.tensor_copy(sa_v[:], modfm[:, 0:8]))
            S.bar(v.tensor_scalar(onca[:], modfm[:, 8:16], 1.0, None, ALU.add))
            S.bar(v.tensor_copy(sbv[:], modfm[:, 24:32]))
            S.bar(v.tensor_scalar(oncb[:], modfm[:, 32:40], 1.0, None, ALU.add))
            for src, dst in ((16, ga_v), (40, gb_v)):
                g0 = modfm[:, src:src + 8]
                S.bar(v.tensor_scalar(tmp8[:, 0:8], g0, 0.0, 2.0, ALU.is_ge, ALU.mult))
                S.bar(v.tensor_scalar(tmp8[:, 0:8], tmp8[:, 0:8], -1.0, None, ALU.add))
                S.bar(v.tensor_tensor(tmp8[:, 8:16], g0, tmp8[:, 0:8], ALU.mult))
                S.bar(v.tensor_scalar(tmp8[:, 8:16], tmp8[:, 8:16], 0.25, None, ALU.max))
                inst = v.tensor_tensor(dst[:], tmp8[:, 0:8], tmp8[:, 8:16], ALU.mult)
            return inst
        A("VE", modpost, wait=("modev5",), tag="modpost")

        if PRO < 4:
            def _dfin(gp):
                for kk in tuple(S.dsems):
                    if S.cnt[kk]:
                        gp.wait_ge(S.sems[kk], S.cnt[kk])
                return gp.engine_nop()
            A("GP", _dfin)
            S.emit(E(nc.Block()))
            return nc

        def gatr(t):
            return t.transpose(ps4[0:8, 0:128], ga_v[:], ident32[:])
        A("TE", gatr, wait=("modpost", pfree["ps4"]), tag="gatr")
        pfree["ps4"] = "gat8"

        def gat8(v):
            return v.tensor_copy(r22[0:8, :], ps4[0:8, 0:128])
        A("VE", gat8, wait=("gatr",), tag="gat8")

        if PRO < 5:
            def _dfin(gp):
                for kk in tuple(S.dsems):
                    if S.cnt[kk]:
                        gp.wait_ge(S.sems[kk], S.cnt[kk])
                return gp.engine_nop()
            A("GP", _dfin)
            S.emit(E(nc.Block()))
            return nc

        GM = int(os.environ.get("KGM", "0"))
        if GM == 3:
            S.alias("garoww", "gat8"); S.alias("garow", "gat8")
        else:
            @dma(1)
            def garow_w(gp):
                return gp.dma_start(out=rowb_h[0:1, 0:1024], in_=r22[0:8, :])
            A("GP", garow_w, wait=("gat8",), tag="garoww", kind="d", dsem="BNC")
            if GM == 2:
                S.alias("garow", "garoww")
            else:
                gdst = sbt("garow_ded", [1, 1024]).ap() if GM == 1 else garow

                @dma(1)
                def garow_d(gp, gdst=gdst):
                    return gp.dma_start(out=gdst, in_=rowb_h[0:1, 0:1024])
                A("GP", garow_d, wait=("garoww",), tag="garow", kind="d", dsem="BNC")

        if PRO < 6:
            def _dfin(gp):
                for kk in tuple(S.dsems):
                    if S.cnt[kk]:
                        gp.wait_ge(S.sems[kk], S.cnt[kk])
                return gp.engine_nop()
            A("GP", _dfin)
            S.emit(E(nc.Block()))
            return nc

        def gabcm(t):
            t.matmul(ps4[:, 0:512], ones1[:], garow[0:1, 0:512], start=True, stop=True)
            return t.matmul(ps5[:, 0:512], ones1[:], garow[0:1, 512:1024], start=True, stop=True)
        A("TE", gabcm, wait=("garow", pfree["ps4"], pfree["ps5"]), tag="gabcm")
        pfree["ps4"] = "gabc"
        pfree["ps5"] = "gabc"

        def gabc(v):
            v.tensor_copy(ga_bc[:, 0:512], ps4[:])
            return v.tensor_copy(ga_bc[:, 512:1024], ps5[:])
        A("VE", gabc, wait=("gabcm",), tag="gabc")

        if DBG < 2:
            if DUMP:
                emit_dumps()
            snap2 = dict(S.cnt)

            def _dfin(gp, snap2=snap2):
                for kk in tuple(S.dsems):
                    if snap2.get(kk, 0):
                        gp.wait_ge(S.sems[kk], snap2.get(kk, 0))
                return gp.engine_nop()
            A("GP", _dfin)
            S.emit(E(nc.Block()))
            return nc

        def emit_dumps():
            dmp = {}
            for name, shape, dt in (
                    ("d_modfm", [128, 48], F32), ("d_hbuf", [128, 4096], BF16),
                    ("d_qsb", [128, 4096], BF16), ("d_ksb", [128, 4096], BF16),
                    ("d_ktb0", [128, 1024], BF16), ("d_vtb0", [128, 1024], BF16),
                    ("d_ybig", [128, 8 * T], BF16), ("d_Sa", [128, 1024], BF16),
                    ("d_Sb", [128, 1024], BF16), ("d_zb0", [128, 1024], BF16),
                    ("d_scal", [1, 16], F32), ("d_scal2", [1, 16], F32),
                    ("d_mxd", [128, NIC * NTT], F32), ("d_col1", [128, 1], F32),
                    ("d_mxb", [128, NCH], F32)):
                dmp[name] = nc.dram_tensor(name, shape, dt, kind="ExternalOutput").ap()

            snap = dict(S.cnt)

            def drain(gp, snap=snap):
                for kk in tuple(S.dsems) + ("VEc", "SCc", "TEc", "GPc"):
                    if snap.get(kk, 0):
                        gp.wait_ge(S.sems[kk], snap.get(kk, 0))
                return gp.engine_nop()
            A("GP", drain, tag="draindump")

            @dma(15)
            def dumps(gp):
                r = []
                for ap, src in ((dmp["d_modfm"], modfm[:]), (dmp["d_hbuf"], hbf()[:]),
                                (dmp["d_qsb"], qsb()[:]), (dmp["d_ksb"], ksb()[:]),
                                (dmp["d_ktb0"], ktb[0][:]), (dmp["d_vtb0"], vtb[0][:, 0:1024]),
                                (dmp["d_ybig"], ybig[:]), (dmp["d_Sa"], S_all[0][:]),
                                (dmp["d_Sb"], S_all[1][:]), (dmp["d_zb0"], zbuf[0][:]),
                                (dmp["d_scal"], scal[:]), (dmp["d_scal2"], scal2[:]),
                                (dmp["d_mxd"], mxd[:]), (dmp["d_col1"], col1[:]),
                                (dmp["d_mxb"], mxb[:])):
                    r.append(gp.dma_start(out=ap, in_=src))
                return r
            A("GP", dumps, wait=("draindump",), tag="dumps", kind="d")

        def emit_rowmax(tag, nat_h, nrow, ncol, mfm):
            ntile = nrow // 128
            nch = (ncol + 1023) // 1024
            k = 0
            for tix in range(ntile):
                for chx in range(nch):
                    b = k % 2
                    c0 = chx * 1024
                    cw = min(1024, ncol - c0)
                    yield

                    @dma(1)
                    def natld(gp, tix=tix, b=b, nat_h=nat_h, c0=c0, cw=cw):
                        return gp.dma_start(out=wna[b]()[:, 0:cw],
                                            in_=nat_h[tix * 128:(tix + 1) * 128, c0:c0 + cw])
                    A("GP", natld, wait=("ldc",) + ((f"{tag}rm{k-2}",) if k >= 2 else ()),
                      tag=f"{tag}nat{k}", kind="d", dsem=f"{tag}NA{b}")

                    def rmax(v, tix=tix, b=b, mfm=mfm, cw=cw, chx=chx):
                        if chx == 0:
                            return v.tensor_reduce(mfm[:, tix:tix + 1], wna[b]()[:, 0:cw],
                                                   AX, ALU.max, apply_absolute_value=True)
                        S.bar(v.tensor_reduce(col1[:], wna[b]()[:, 0:cw],
                                              AX, ALU.max, apply_absolute_value=True))
                        return v.tensor_tensor(mfm[:, tix:tix + 1], mfm[:, tix:tix + 1],
                                               col1[:], ALU.max)
                    A("VE", rmax, wait=(f"{tag}nat{k}",), tag=f"{tag}rm{k}")
                    k += 1
            S.alias(f"{tag}rmdone", f"{tag}rm{k-1}")
            yield

        def emit_rbc(tag, mfm, ntile, width, prev=None):
            yield
            def rfm(v, mfm=mfm, ntile=ntile):
                S.bar(v.reciprocal(gnst[:, 0:ntile], mfm[:, 0:ntile]))
                return v.tensor_scalar(gnst[:, 0:ntile], gnst[:, 0:ntile], QMAX, None, ALU.mult)
            A("VE", rfm, wait=(f"{tag}rmdone",)
              + ((f"{prev}rtr",) if prev else ()), tag=f"{tag}rfm")

            def rtr(t, ntile=ntile):
                return t.transpose(ps4[0:ntile, 0:128], gnst[:, 0:ntile], ident32[:])
            A("TE", rtr, wait=(f"{tag}rfm", pfree["ps4"]), tag=f"{tag}rtr")
            pfree["ps4"] = f"{tag}r22"

            def r22e(v, ntile=ntile):
                return v.tensor_copy(r22[0:ntile, :], ps4[0:ntile, 0:128])
            A("VE", r22e, wait=(f"{tag}rtr",)
              + ((f"{prev}rrow",) if prev else ("garow",)), tag=f"{tag}r22")

            @dma(1)
            def rroww(gp, ntile=ntile, width=width):
                return gp.dma_start(out=rowb_h[0:1, 0:width], in_=r22[0:ntile, :])
            A("GP", rroww, wait=(f"{tag}r22", "garow"), tag=f"{tag}rroww", kind="d", dsem="BNC")

            @dma(1)
            def rrowd(gp, width=width):
                return gp.dma_start(out=rrow[0:1, 0:width], in_=rowb_h[0:1, 0:width])
            A("GP", rrowd, wait=(f"{tag}rroww",), tag=f"{tag}rrow", kind="d", dsem="BNC")

            nmm = (width + 511) // 512
            for m in range(nmm):
                c0 = m * 512
                cw = min(512, width - c0)
                pn = "ps4" if m % 2 == 0 else "ps5"
                pst = ps4 if m % 2 == 0 else ps5

                def rbcm(t, pst=pst, c0=c0, cw=cw):
                    return t.matmul(pst[:, 0:cw], ones1[:], rrow[0:1, c0:c0 + cw],
                                    start=True, stop=True)
                A("TE", rbcm, wait=(f"{tag}rrow", pfree[pn]), tag=f"{tag}rbcm{m}")
                pfree[pn] = f"{tag}rbc{m}"

                def rbce(v, pst=pst, c0=c0, cw=cw):
                    return v.tensor_copy(rbc()[:, c0:c0 + cw], pst[:, 0:cw])
                A("VE", rbce, wait=(f"{tag}rbcm{m}",), tag=f"{tag}rbc{m}")
            S.alias(f"{tag}rbc", f"{tag}rbc{nmm-1}")
            yield

        def emit_wquant(tag, wT_h, width, st_h):
            nrows = wT_h.shape[0]
            nfc = nrows // 128
            halfw = width // 2 if width > 1408 else width
            nhh = width // halfw
            nk = nfc * nhh
            for fc in range(nfc):
                for hh in range(nhh):
                    k = fc * nhh + hh
                    b = k % 2
                    c0 = hh * halfw
                    yield

                    @dma(1)
                    def wl(gp, fc=fc, c0=c0, b=b, wT_h=wT_h, halfw=halfw):
                        return gp.dma_start(out=wqa[b]()[:, 0:halfw],
                                            in_=wT_h[fc * 128:(fc + 1) * 128, c0:c0 + halfw])
                    A("GP", wl, wait=((f"{tag}q3_{k-2}",) if k >= 2 else ()) + (f"{tag}rbc",),
                      tag=f"{tag}wl{k}", kind="d", dsem=f"{tag}WL{b}")

                    def q3(v, k=k, b=b, c0=c0, halfw=halfw):
                        S.bar(v.tensor_tensor(wqa[b]()[:, 0:halfw], wqa[b]()[:, 0:halfw],
                              rbc()[:, c0:c0 + halfw], ALU.mult))
                        S.bar(v.tensor_scalar(wqa[b]()[:, 0:halfw], wqa[b]()[:, 0:halfw],
                              MAGIC, MAGIC - 128.0, ALU.add, ALU.max))
                        return v.tensor_scalar(iqa[b]()[:, 0:halfw], wqa[b]()[:, 0:halfw],
                                               MAGIC + QMAX, MAGIC, ALU.min, ALU.subtract)
                    A("VE", q3, wait=(f"{tag}wl{k}", f"{tag}rbc")
                      + ((f"{tag}st{k-2}",) if k >= 2 else ()), tag=f"{tag}q3_{k}")

                    @dma(1)
                    def stq(gp, fc=fc, hh=hh, b=b, st_h=st_h, halfw=halfw):
                        if st_h is idst_h:
                            dst = st_h[fc]
                            return gp.dma_start(out=dst, in_=iqa[b]()[:, 0:halfw])
                        nt = halfw // 128
                        i0 = hh * nt
                        return gp.dma_start(out=st_h[i0:i0 + nt, fc].rearrange("i p n -> p i n"),
                                            in_=iqa[b]()[:, 0:halfw]
                                            .rearrange("p (i n) -> p i n", n=128))
                    A("GP", stq, wait=(f"{tag}q3_{k}",), tag=f"{tag}st{k}", kind="d",
                      dsem=f"{tag}QS{b}")
            S.alias(f"{tag}qdone", f"{tag}st{nk-1}")
            S.alias(f"{tag}qdoneB", f"{tag}st{nk-2}")
            yield


        import itertools
        quant_gen = itertools.chain(
            emit_rowmax("g", wg_h, I, D, mg_fm),
            emit_rbc("g", mg_fm, NIC, I),
            emit_wquant("g", wgT_h, I, igst_h),
            emit_rowmax("u", wu_h, I, D, mu_fm),
            emit_rbc("u", mu_fm, NIC, I, prev="g"),
            emit_wquant("u", wuT_h, I, iust_h),
            emit_rowmax("d", wd_h, D, I, md_fm),
            emit_rbc("d", md_fm, 8, D, prev="u"),
            emit_wquant("d", wdT_h, D, idst_h))

        def run_quant_chunk():
            n = max(1, 210 // max(NCH, 1))
            for _ in range(n + 2):
                next(quant_gen, None)

        for g in range(NG):
            for i4 in range(4):
                tix = g * 4 + i4
                b = tix % 2

                @dma(1)
                def xld(sy, t0=g * 512 + i4 * 128, b4=tix % 4):
                    return sy.dma_start(out=xbuf[b4], in_=x_h[t0:t0 + 128, :])
                A("SY", xld, wait=((f"z{tix-4}",) if tix >= 4 else ("ldc",)),
                  tag=f"xld{tix}", kind="d", dsem=f"XB{tix % 4}")

            def stats_f(v, g=g):
                for i4 in range(4):
                    S.bar(v.bn_stats(stats[:, i4 * 12:i4 * 12 + 6], xbuf[(g * 4 + i4) % 4][:, 0:512]))
                    S.bar(v.bn_stats(stats[:, i4 * 12 + 6:i4 * 12 + 12], xbuf[(g * 4 + i4) % 4][:, 512:1024]))
                mean = stat2[:, 0:4]; var = stat2[:, 4:8]
                S.bar(v.tensor_tensor(stats[:, 48:52], stats[:, 1:48:12], stats[:, 4:48:12], ALU.add))
                S.bar(v.tensor_tensor(stats[:, 52:56], stats[:, 7:48:12], stats[:, 10:48:12], ALU.add))
                S.bar(v.tensor_tensor(mean, stats[:, 48:52], stats[:, 52:56], ALU.add))
                S.bar(v.tensor_scalar(mean, mean, 0.25, None, ALU.mult))
                S.bar(v.tensor_tensor(stats[:, 48:52], stats[:, 2:48:12], stats[:, 5:48:12], ALU.add))
                S.bar(v.tensor_tensor(stats[:, 52:56], stats[:, 8:48:12], stats[:, 11:48:12], ALU.add))
                S.bar(v.tensor_tensor(var, stats[:, 48:52], stats[:, 52:56], ALU.add))
                for kk, c0 in enumerate((1, 4, 7, 10)):
                    S.bar(v.tensor_tensor(stats[:, 48 + 4 * kk:52 + 4 * kk],
                                          stats[:, c0:48:12], stats[:, c0:48:12], ALU.mult))
                S.bar(v.tensor_tensor(stats[:, 48:52], stats[:, 48:52], stats[:, 52:56], ALU.add))
                S.bar(v.tensor_tensor(stats[:, 56:60], stats[:, 56:60], stats[:, 60:64], ALU.add))
                S.bar(v.tensor_tensor(tmp8[:, 0:4], stats[:, 48:52], stats[:, 56:60], ALU.add))
                S.bar(v.scalar_tensor_tensor(var, tmp8[:, 0:4], 256.0, var, ALU.mult, ALU.add))
                S.bar(v.tensor_scalar(var, var, 1.0 / 1024.0, None, ALU.mult))
                S.bar(v.tensor_tensor(tmp8[:, 0:4], mean, mean, ALU.mult))
                return v.tensor_tensor(var, var, tmp8[:, 0:4], ALU.subtract)
            A("VE", stats_f, wait=tuple(f"xld{g*4+i}" for i in range(4)), tag=f"l1v{g}")

            def sqrt_f(sc, g=g):
                return sc.activation(stat2[:, 4:8], stat2[:, 4:8], ACT.Sqrt, bias=epsv[:])
            A("SC", sqrt_f, wait=(f"l1v{g}",), tag=f"l1q{g}")

            def rstd_f(v, g=g):
                S.bar(v.reciprocal(stat2[:, 8:12], stat2[:, 4:8]))
                S.bar(v.tensor_tensor(stat2[:, 12:16], stat2[:, 0:4], stat2[:, 8:12], ALU.mult))
                return v.tensor_scalar(stat2[:, 12:16], stat2[:, 12:16], -1.0, None, ALU.mult)
            A("VE", rstd_f, wait=(f"l1q{g}",), tag=f"l1r{g}")

            for i4 in range(4):
                tix = g * 4 + i4
                b = tix % 2

                def z_f(sc, i4=i4, b=b, b4=tix % 4):
                    return sc.activation(zbuf[b][:], xbuf[b4], ACT.Identity,
                                         scale=stat2[:, 8 + i4:9 + i4],
                                         bias=stat2[:, 12 + i4:13 + i4])
                A("SC", z_f, wait=(f"l1r{g}",) + ((f"htro{tix-2}",) if tix >= 2 and PA >= 2 else ()),
                  tag=f"z{tix}")
                if PA < 2:
                    continue

                def htre(t, tix=tix, b=b):
                    for fc in (0, 2, 4, 6):
                        inst = t.transpose(ps0b[:, (fc // 2) * 128:(fc // 2) * 128 + 128],
                                           zbuf[b][:, fc * 128:(fc + 1) * 128], identbf[:])
                    return inst
                A("TE", htre, wait=(f"z{tix}", "modpost", pfree["ps0"]), tag=f"htre{tix}")
                pfree["ps0"] = f"heve{tix}"

                def htro(t, tix=tix, b=b):
                    for fc in (1, 3, 5, 7):
                        inst = t.transpose(ps1b[:, (fc // 2) * 128:(fc // 2) * 128 + 128],
                                           zbuf[b][:, fc * 128:(fc + 1) * 128], identbf[:])
                    return inst
                A("TE", htro, wait=(f"z{tix}", "modpost", pfree["ps1"]), tag=f"htro{tix}")
                pfree["ps1"] = f"hevo{tix}"

                def heve(sc, tix=tix, i4=i4):
                    for fc in (0, 2, 4, 6):
                        inst = sc.activation(
                            hbf()[:, fc * 512 + i4 * 128:fc * 512 + i4 * 128 + 128],
                            ps0b[:, (fc // 2) * 128:(fc // 2) * 128 + 128], ACT.Identity,
                            bias=sa_v[:, fc:fc + 1], scale=onca[:, fc:fc + 1])
                    return inst
                A("SC", heve, wait=(f"htre{tix}",), tag=f"heve{tix}")

                def hevo(sc, tix=tix, i4=i4):
                    for fc in (1, 3, 5, 7):
                        inst = sc.activation(
                            hbf()[:, fc * 512 + i4 * 128:fc * 512 + i4 * 128 + 128],
                            ps1b[:, (fc // 2) * 128:(fc // 2) * 128 + 128], ACT.Identity,
                            bias=sa_v[:, fc:fc + 1], scale=onca[:, fc:fc + 1])
                    return inst
                A("SC", hevo, wait=(f"htro{tix}",), tag=f"hevo{tix}")

            hdone = tuple(f"hev{p}{g*4+i}" for p in ("e", "o") for i in range(4))

            for oc in range(8):
                if PA < 3:
                    break
                b = oc % 2
                prev = (f"qkev{g}_{oc-2}" if oc >= 2
                        else (f"qkev{g-1}_{6+oc}" if g > 0 else None))

                @dma(2)
                def wqk_ld(sy, oc=oc, b=b):
                    return [
                        sy.dma_start(out=wtq[b].rearrange("p (fc o) -> p fc o", fc=8),
                                     in_=wbf_h[0][:, oc * 128:(oc + 1) * 128]
                                     .rearrange("(fc p) o -> p fc o", p=128)),
                        sy.dma_start(out=wtk[b].rearrange("p (fc o) -> p fc o", fc=8),
                                     in_=wbf_h[1][:, oc * 128:(oc + 1) * 128]
                                     .rearrange("(fc p) o -> p fc o", p=128))]
                A("SY", wqk_ld, wait=("wstage", "modmm5") + ((prev,) if prev else ()),
                  tag=f"wqk{g}_{oc}", kind="d", dsem=f"QKW{b}")

                def qkmm(t, oc=oc, b=b):
                    for fc in range(8):
                        t.matmul(ps0[:], wtq[b][:, fc * 128:(fc + 1) * 128],
                                 hbf()[:, fc * 512:(fc + 1) * 512],
                                 start=(fc == 0), stop=(fc == 7))
                    for fc in range(8):
                        inst = t.matmul(ps1[:], wtk[b][:, fc * 128:(fc + 1) * 128],
                                        hbf()[:, fc * 512:(fc + 1) * 512],
                                        start=(fc == 0), stop=(fc == 7))
                    return inst
                A("TE", qkmm, wait=hdone + (f"wqk{g}_{oc}", pfree["ps0"], pfree["ps1"]),
                  tag=f"qkmm{g}_{oc}")
                pfree["ps0"] = f"qkev{g}_{oc}"
                pfree["ps1"] = f"qkev{g}_{oc}"

                def qkev(v, oc=oc, g=g):
                    for cl in range(4):
                        v.tensor_tensor(qsb()[:, oc * 512 + cl * 128:oc * 512 + cl * 128 + 128],
                                        ps0[:, cl * 128:(cl + 1) * 128],
                                        qscale[:, oc * 128:(oc + 1) * 128], ALU.mult)
                    for cl in range(4):
                        inst = v.tensor_tensor(ksb()[:, oc * 512 + cl * 128:oc * 512 + cl * 128 + 128],
                                               ps1[:, cl * 128:(cl + 1) * 128],
                                               kscale[:, oc * 128:(oc + 1) * 128], ALU.mult)
                    return inst
                A("VE", qkev, wait=(f"qkmm{g}_{oc}",), tag=f"qkev{g}_{oc}")

            for half in range(2):
                if PA < 4:
                    break
                if half == 0:
                    prevw = f"vtmm{g-1}_3_1" if g > 0 else None
                else:
                    prevw = f"vtmm{g}_3_0"

                @dma(1)
                def wv_ld(sy, half=half):
                    return sy.dma_start(out=wvw().rearrange("p (fc o) -> p fc o", fc=8),
                                        in_=wbf_h[2][:, half * 512:(half + 1) * 512]
                                        .rearrange("(fc p) o -> p fc o", p=128))
                A("SY", wv_ld, wait=("wstage",) + ((prevw,) if prevw else ()),
                  tag=f"wv{g}_{half}", kind="d", dsem="WVL")

                for cc in range(4):
                    ci = g * 4 + cc
                    pv, pvn = (ps2, "ps2") if cc % 2 == 0 else (ps3, "ps3")
                    bre = ((f"smm{(g-1)*4+cc}_7", f"ymm{(g-1)*4+cc}_7") if g > 0 else ())

                    def vtmm(t, cc=cc, half=half, pv=pv):
                        for fc in range(8):
                            inst = t.matmul(pv[:],
                                            hbf()[:, fc * 512 + cc * 128:fc * 512 + cc * 128 + 128],
                                            wvw()[:, fc * 512:(fc + 1) * 512],
                                            start=(fc == 0), stop=(fc == 7))
                        return inst
                    A("TE", vtmm, wait=hdone + (f"wv{g}_{half}", pfree[pvn]),
                      tag=f"vtmm{g}_{cc}_{half}")
                    pfree[pvn] = f"vtev{g}_{cc}_{half}"

                    def vtev(sc, cc=cc, half=half, pv=pv):
                        return sc.activation(vtb[cc][:, half * 512:(half + 1) * 512],
                                             pv[:], ACT.Copy)
                    A("SC", vtev, wait=(f"vtmm{g}_{cc}_{half}",) + bre,
                      tag=f"vtev{g}_{cc}_{half}")

            for cc in range(4):
                if PA < 5:
                    break
                ci = g * 4 + cc
                bb = ci % 2
                scur = S_all[ci % 2]
                snew = S_all[(ci + 1) % 2]
                sprev = f"sev{ci-1}" if ci > 0 else "init0"
                qkall = (f"qkev{g}_7",)

                kbre = (f"smm{ci-2}_7",) if (ci >= 2 and PA >= 8) else ()

                def ktr_e(t, cc=cc):
                    for oc in (0, 2, 4, 6):
                        inst = t.transpose(ps2b[:, (oc // 2) * 128:(oc // 2) * 128 + 128],
                                           ksb()[:, oc * 512 + cc * 128:oc * 512 + cc * 128 + 128],
                                           identbf[:])
                    return inst
                A("TE", ktr_e, wait=qkall + (pfree["ps2"],), tag=f"ktre{ci}")
                pfree["ps2"] = f"kteve{ci}"

                def ktr_o(t, cc=cc):
                    for oc in (1, 3, 5, 7):
                        inst = t.transpose(ps3b[:, (oc // 2) * 128:(oc // 2) * 128 + 128],
                                           ksb()[:, oc * 512 + cc * 128:oc * 512 + cc * 128 + 128],
                                           identbf[:])
                    return inst
                A("TE", ktr_o, wait=qkall + (pfree["ps3"],), tag=f"ktro{ci}")
                pfree["ps3"] = f"ktevo{ci}"

                KTEV = int(os.environ.get("KKTEV", "0"))
                if KTEV == 9:
                    S.alias(f"kteve{ci}", f"ktre{ci}")
                    S.alias(f"ktevo{ci}", f"ktro{ci}")

                def ktev_e(sc, bb=bb, KTEV=KTEV):
                    if KTEV == 1:
                        for q, oc in enumerate((0, 2, 4, 6)):
                            inst = sc.activation(ktb[bb][:, oc * 128:(oc + 1) * 128],
                                                 ps2b[:, q * 128:(q + 1) * 128], ACT.Copy)
                        return inst
                    return sc.activation(
                        ktb[bb].rearrange("p (o n) -> p o n", n=128)[:, 0:8:2, :],
                        ps2b[:, 0:512].rearrange("p (q n) -> p q n", q=4), ACT.Copy)
                if KTEV != 9:
                    A("SC", ktev_e, wait=(f"ktre{ci}",) + kbre, tag=f"kteve{ci}")

                def ktev_o(sc, bb=bb, KTEV=KTEV):
                    if KTEV == 1:
                        for q, oc in enumerate((1, 3, 5, 7)):
                            inst = sc.activation(ktb[bb][:, oc * 128:(oc + 1) * 128],
                                                 ps3b[:, q * 128:(q + 1) * 128], ACT.Copy)
                        return inst
                    return sc.activation(
                        ktb[bb].rearrange("p (o n) -> p o n", n=128)[:, 1:8:2, :],
                        ps3b[:, 0:512].rearrange("p (q n) -> p q n", q=4), ACT.Copy)
                if KTEV != 9:
                    A("SC", ktev_o, wait=(f"ktro{ci}",) + kbre, tag=f"ktevo{ci}")

                kvdone = (f"kteve{ci}", f"ktevo{ci}",
                          f"vtev{g}_{cc}_0", f"vtev{g}_{cc}_1")
                if PA < 6:
                    continue

                for hp in range(8):
                    pe_, pen = (ps0, "ps0") if hp % 2 == 0 else (ps2, "ps2")
                    po_, pon = (ps1, "ps1") if hp % 2 == 0 else (ps3, "ps3")

                    def amm(t, cc=cc, hp=hp, pe_=pe_, po_=po_):
                        sl = slice(hp * 512 + cc * 128, hp * 512 + cc * 128 + 128)
                        t.matmul(pe_[:, 0:128], ksb()[0:64, sl], qsb()[0:64, sl],
                                 start=True, stop=True)
                        return t.matmul(po_[:, 0:128], ksb()[64:128, sl], qsb()[64:128, sl],
                                        start=True, stop=True)
                    A("TE", amm, wait=qkall + (pfree[pen], pfree[pon]), tag=f"amm{ci}_{hp}")
                    pfree[pen] = f"amev{ci}_{hp}"
                    pfree[pon] = f"amev{ci}_{hp}"

                    if PA < 7:
                        ambre = ()
                    elif hp >= 2:
                        ambre = (f"ymm{ci}_{hp-2}",)
                    elif ci > 0:
                        ambre = (f"ymm{ci-1}_{hp+6}",)
                    else:
                        ambre = ()

                    def amev(v, ci=ci, hp=hp, pe_=pe_, po_=po_):
                        v.tensor_tensor(amb[hp % 2][:, 0:128], pe_[:, 0:128],
                                        causal2[:, 0:128], ALU.mult)
                        return v.tensor_tensor(amb[hp % 2][:, 128:256], po_[:, 0:128],
                                               causal2[:, 128:256], ALU.mult)
                    A("VE", amev, wait=(f"amm{ci}_{hp}",) + ambre, tag=f"amev{ci}_{hp}")

                    py, pyn = (ps4, "ps4") if hp < 4 else (ps5, "ps5")
                    col = (hp % 4) * 128

                    def ymm(t, ci=ci, cc=cc, hp=hp, py=py, col=col, scur=scur):
                        sl = slice(hp * 512 + cc * 128, hp * 512 + cc * 128 + 128)
                        t.matmul(py[0:64, col:col + 128],
                                 vtb[cc][:, hp * 128:hp * 128 + 64],
                                 amb[hp % 2][:, 0:128], start=True, stop=False)
                        t.matmul(py[0:64, col:col + 128], scur[:, 64 * hp:64 * hp + 64],
                                 qsb()[:, sl], start=False, stop=True)
                        t.matmul(py[64:128, col:col + 128],
                                 vtb[cc][:, hp * 128 + 64:hp * 128 + 128],
                                 amb[hp % 2][:, 128:256], start=True, stop=False,
                                 tile_position=(0, 64))
                        return t.matmul(py[64:128, col:col + 128],
                                        scur[:, 512 + 64 * hp:512 + 64 * hp + 64],
                                        qsb()[:, sl], start=False, stop=True,
                                        tile_position=(0, 64))
                    if PA < 7:
                        continue
                    sprev_y = sprev if PA >= 8 else "init0"
                    A("TE", ymm, wait=(f"amev{ci}_{hp}", sprev_y, pfree[pyn]) + kvdone,
                      tag=f"ymm{ci}_{hp}")
                    if hp % 4 == 3:
                        pfree[pyn] = f"yev{ci}_{hp//4}"

                    def smm(t, ci=ci, cc=cc, hp=hp, bb=bb, scur=scur):
                        t.matmul(psS[0:64, 64 * hp:64 * hp + 64],
                                 ktb[bb][:, hp * 128:hp * 128 + 64],
                                 vtb[cc][:, hp * 128:hp * 128 + 64], start=True, stop=False)
                        t.matmul(psS[0:64, 64 * hp:64 * hp + 64],
                                 identbf[:, 0:64], scur[:, 64 * hp:64 * hp + 64],
                                 start=False, stop=True)
                        t.matmul(psS[64:128, 512 + 64 * hp:512 + 64 * hp + 64],
                                 ktb[bb][:, hp * 128 + 64:hp * 128 + 128],
                                 vtb[cc][:, hp * 128 + 64:hp * 128 + 128],
                                 start=True, stop=False, tile_position=(0, 64))
                        return t.matmul(psS[64:128, 512 + 64 * hp:512 + 64 * hp + 64],
                                        identbf[:, 64:128],
                                        scur[:, 512 + 64 * hp:512 + 64 * hp + 64],
                                        start=False, stop=True, tile_position=(0, 64))
                    if PA < 8:
                        continue
                    A("TE", smm, wait=kvdone + (sprev, pfree["psS"]), tag=f"smm{ci}_{hp}")

                for bank in range(2 if PA >= 7 else 0):
                    py = ps4 if bank == 0 else ps5

                    def yevb(sc, ci=ci, bank=bank, py=py):
                        return sc.activation(
                            ybig.rearrange("p (hp t) -> p hp t", hp=8)[
                                :, bank * 4:bank * 4 + 4, ci * 128:ci * 128 + 128],
                            py[:, 0:512].rearrange("p (q n) -> p q n", q=4),
                            ACT.Copy)
                    A("SC", yevb, wait=(f"ymm{ci}_{bank*4+3}",), tag=f"yev{ci}_{bank}")

                def sev(v, ci=ci, snew=snew):
                    v.tensor_tensor(snew[0:64, 0:512], psS[0:64, 0:512],
                                    gcscale[0:64, 0:512], ALU.mult)
                    return v.tensor_tensor(snew[64:128, 512:1024], psS[64:128, 512:1024],
                                           gcscale[64:128, 512:1024], ALU.mult)
                if PA >= 8:
                    A("VE", sev, wait=tuple(f"smm{ci}_{hp}" for hp in range(8)), tag=f"sev{ci}")
                    pfree["psS"] = f"sev{ci}"

                run_quant_chunk()

        if DBG < 3:
            if DUMP:
                emit_dumps()
            snap2 = dict(S.cnt)

            def _dfin(gp, snap2=snap2):
                for kk in tuple(S.dsems):
                    if snap2.get(kk, 0):
                        gp.wait_ge(S.sems[kk], snap2.get(kk, 0))
                return gp.engine_nop()
            A("GP", _dfin)
            S.emit(E(nc.Block()))
            return nc



        if DBG < 4:
            if DUMP:
                emit_dumps()
            snap2 = dict(S.cnt)

            def _dfin(gp, snap2=snap2):
                for kk in tuple(S.dsems):
                    if snap2.get(kk, 0):
                        gp.wait_ge(S.sems[kk], snap2.get(kk, 0))
                return gp.engine_nop()
            A("GP", _dfin)
            S.emit(E(nc.Block()))
            return nc

        for _ in iter(lambda: next(quant_gen, StopIteration), StopIteration):
            pass

        retdone = (f"yev{NCH-1}_1", f"sev{NCH-1}")
        for hp in range(8):
            def gnstats(v, hp=hp):
                nb = T // 512
                for k in range(nb):
                    S.bar(v.bn_stats(gnst[:, k * 6:(k + 1) * 6],
                          ybig[:, hp * T + k * 512:hp * T + k * 512 + 512]))
                ncols = nb * 6
                me = gnst[:, 1:ncols:6]; mo = gnst[:, 4:ncols:6]
                m2e = gnst[:, 2:ncols:6]; m2o = gnst[:, 5:ncols:6]
                S.bar(v.tensor_tensor(stats[:, 48:48 + nb], me, mo, ALU.add))
                S.bar(v.tensor_reduce(col1[:], stats[:, 48:48 + nb], AX, ALU.add))
                S.bar(v.tensor_scalar(gnsy[:, hp:hp + 1], col1[:], 256.0, None, ALU.mult))
                S.bar(v.tensor_tensor(stats[:, 48:48 + nb], m2e, m2o, ALU.add))
                S.bar(v.tensor_reduce(col1[:], stats[:, 48:48 + nb], AX, ALU.add))
                S.bar(v.tensor_tensor(stats[:, 48:48 + nb], me, me, ALU.mult))
                S.bar(v.tensor_tensor(stats[:, 56:56 + nb], mo, mo, ALU.mult))
                S.bar(v.tensor_tensor(stats[:, 48:48 + nb], stats[:, 48:48 + nb],
                      stats[:, 56:56 + nb], ALU.add))
                S.bar(v.tensor_reduce(tmp8[:, 0:1], stats[:, 48:48 + nb], AX, ALU.add))
                return v.scalar_tensor_tensor(gnsyy[:, hp:hp + 1], tmp8[:, 0:1], 256.0,
                                              col1[:], ALU.mult, ALU.add)
            A("VE", gnstats, wait=retdone + ("dqdone", "dqdoneB"), tag=f"gnst{hp}")

        def gntr(t):
            t.transpose(ps0[0:8, 0:128], gnsy[:], ident32[:])
            return t.transpose(ps0[0:8, 128:256], gnsyy[:], ident32[:])
        A("TE", gntr, wait=tuple(f"gnst{hp}" for hp in range(8)) + (pfree["ps0"],),
          tag="gntr")
        pfree["ps0"] = "gnhead"

        def gnhead(v):
            S.bar(v.tensor_reduce(hse[:, 0:1], ps0[0:8, 0:64], AX, ALU.add))
            S.bar(v.tensor_reduce(hso[:, 0:1], ps0[0:8, 64:128], AX, ALU.add))
            S.bar(v.tensor_reduce(hse[:, 1:2], ps0[0:8, 128:192], AX, ALU.add))
            S.bar(v.tensor_reduce(hso[:, 1:2], ps0[0:8, 192:256], AX, ALU.add))
            NT = float(64 * T)
            for hs in (hse, hso):
                S.bar(v.tensor_scalar(hs[:, 2:3], hs[:, 0:1], 1.0 / NT, None, ALU.mult))
                S.bar(v.tensor_scalar(hs[:, 3:4], hs[:, 1:2], 1.0 / NT, None, ALU.mult))
                S.bar(v.tensor_tensor(hs[:, 4:5], hs[:, 2:3], hs[:, 2:3], ALU.mult))
                inst = v.tensor_tensor(hs[:, 3:4], hs[:, 3:4], hs[:, 4:5], ALU.subtract)
            return inst
        A("VE", gnhead, wait=("gntr",), tag="gnhead")

        def gnsqrt(sc):
            sc.activation(hse[:, 3:4], hse[:, 3:4], ACT.Sqrt, bias=epsv[0:8, :])
            return sc.activation(hso[:, 3:4], hso[:, 3:4], ACT.Sqrt, bias=epsv[0:8, :])
        A("SC", gnsqrt, wait=("gnhead",), tag="gnsqrt")

        def gnrstd(v):
            for hs in (hse, hso):
                S.bar(v.reciprocal(hs[:, 5:6], hs[:, 3:4]))
                inst = v.tensor_tensor(hs[:, 6:7], hs[:, 2:3], hs[:, 5:6], ALU.mult)
            return inst
        A("VE", gnrstd, wait=("gnsqrt",), tag="gnrstd")

        def gnexp(t):
            for oc in range(8):
                pst = ps0 if oc % 2 == 0 else ps1
                c0 = 2 * (oc // 2)
                t.matmul(pst[:, c0:c0 + 2], sel_e[:, oc * 128:(oc + 1) * 128],
                         hse[:, 5:7], start=True, stop=False)
                inst = t.matmul(pst[:, c0:c0 + 2], sel_o[:, oc * 128:(oc + 1) * 128],
                                hso[:, 5:7], start=False, stop=True)
            return inst
        A("TE", gnexp, wait=("gnrstd", pfree["ps0"], pfree["ps1"]), tag="gnexp")
        pfree["ps0"] = "gnab"
        pfree["ps1"] = "gnab"

        def gnab(v):
            for oc in range(8):
                pst = ps0 if oc % 2 == 0 else ps1
                c0 = 2 * (oc // 2)
                S.bar(v.tensor_copy(tmp8[:, oc:oc + 1], pst[:, c0:c0 + 1]))
                S.bar(v.tensor_copy(tmp8[:, 8 + oc:9 + oc], pst[:, c0 + 1:c0 + 2]))
            S.bar(v.tensor_tensor(gnA[:], tmp8[:, 0:8], gnwv[:], ALU.mult))
            S.bar(v.tensor_tensor(gnB[:], tmp8[:, 8:16], gnwv[:], ALU.mult))
            return v.tensor_tensor(gnB[:], gnbv[:], gnB[:], ALU.subtract)
        A("VE", gnab, wait=("gnexp",), tag="gnab")

        @dma(2)
        def wo_ld(sy):
            return [sy.dma_start(out=wkw().rearrange("p (fc o) -> p fc o", fc=8),
                                 in_=wbf_h[3][:, 0:512].rearrange("(fc p) o -> p fc o", p=128)),
                    sy.dma_start(out=wvw().rearrange("p (fc o) -> p fc o", fc=8),
                                 in_=wbf_h[3][:, 512:1024].rearrange("(fc p) o -> p fc o", p=128))]
        A("SY", wo_ld, wait=(f"vtmm{NG-1}_3_1", "wstage", f"qkmm{NG-1}_7", f"ymm{NCH-1}_7", f"smm{NCH-1}_7"), tag="wo_ld", kind="d", dsem="WOL")

        for ci in range(NCH):
            b = ci % 2

            def ynf(v, ci=ci):
                for hp in range(8):
                    inst = v.tensor_scalar(ynbs[ci % 2][:, hp * 128:(hp + 1) * 128],
                                           ybig[:, hp * T + ci * 128:hp * T + ci * 128 + 128],
                                           gnA[:, hp:hp + 1], gnB[:, hp:hp + 1],
                                           ALU.mult, ALU.add)
                return inst
            A("VE", ynf, wait=("gnab",) + ((f"womm{ci-2}",) if ci > 1 else ()),
              tag=f"yn{ci}")

            @dma(1)
            def x2ld(sy, ci=ci, b4=ci % 4):
                return sy.dma_start(out=xbuf[b4], in_=x_h[ci * 128:(ci + 1) * 128, :])
            A("SY", x2ld, wait=((f"resid{ci-4}",) if ci >= 4 else (f"z{NCH-1}",)),
              tag=f"x2ld{ci}", kind="d", dsem=f"XB{ci % 4}")

            pw0, pw0n = (ps0, "ps0") if ci % 2 == 0 else (ps4, "ps4")
            pw1, pw1n = (ps1, "ps1") if ci % 2 == 0 else (ps5, "ps5")

            def womm(t, ci=ci, pw0=pw0, pw1=pw1):
                for fc in range(8):
                    t.matmul(pw0[:], ynbs[ci % 2][:, fc * 128:(fc + 1) * 128],
                             wkw()[:, fc * 512:(fc + 1) * 512],
                             start=(fc == 0), stop=(fc == 7))
                for fc in range(8):
                    inst = t.matmul(pw1[:], ynbs[ci % 2][:, fc * 128:(fc + 1) * 128],
                                    wvw()[:, fc * 512:(fc + 1) * 512],
                                    start=(fc == 0), stop=(fc == 7))
                return inst
            A("TE", womm, wait=(f"yn{ci}", "wo_ld", pfree[pw0n], pfree[pw1n]),
              tag=f"womm{ci}")
            pfree[pw0n] = f"residv{ci}"
            pfree[pw1n] = f"residv{ci}"

            def residv(v, ci=ci, b=b, pw0=pw0, pw1=pw1):
                v.tensor_tensor(x1b[b]()[:, 0:512], pw0[:], ga_bc[:, 0:512], ALU.mult)
                return v.tensor_tensor(x1b[b]()[:, 512:1024], pw1[:], ga_bc[:, 512:1024], ALU.mult)
            A("VE", residv, wait=(f"womm{ci}", "gabc")
              + ((f"x1st{ci-2}", f"l2z{ci-2}") if ci >= 2 else ()), tag=f"residv{ci}")

            def resid(gp, ci=ci, b=b):
                return gp.tensor_tensor(x1b[b]()[:, 0:1024], x1b[b]()[:, 0:1024],
                                        xbuf[ci % 4], ALU.add)
            A("GP", resid, wait=(f"residv{ci}", f"x2ld{ci}"), tag=f"resid{ci}")

            @dma(1)
            def x1st(sc, ci=ci, b=b):
                return sc.dma_start(out=x1sp_h[ci], in_=x1b[b]()[:, 0:1024])
            A("SC", x1st, wait=(f"resid{ci}",), tag=f"x1st{ci}", kind="d", dsem=f"XS{ci % 2}")

            def l2stats(v, ci=ci, b=b):
                S.bar(v.bn_stats(stats[:, 0:6], x1b[b]()[:, 0:512]))
                S.bar(v.bn_stats(stats[:, 6:12], x1b[b]()[:, 512:1024]))
                mean = stat2[:, 0:1]; var = stat2[:, 4:5]
                S.bar(v.tensor_tensor(stats[:, 48:49], stats[:, 1:2], stats[:, 4:5], ALU.add))
                S.bar(v.tensor_tensor(stats[:, 49:50], stats[:, 7:8], stats[:, 10:11], ALU.add))
                S.bar(v.tensor_tensor(mean, stats[:, 48:49], stats[:, 49:50], ALU.add))
                S.bar(v.tensor_scalar(mean, mean, 0.25, None, ALU.mult))
                S.bar(v.tensor_tensor(stats[:, 50:51], stats[:, 2:3], stats[:, 5:6], ALU.add))
                S.bar(v.tensor_tensor(stats[:, 51:52], stats[:, 8:9], stats[:, 11:12], ALU.add))
                S.bar(v.tensor_tensor(var, stats[:, 50:51], stats[:, 51:52], ALU.add))
                for kk, c0 in enumerate((1, 4, 7, 10)):
                    S.bar(v.tensor_tensor(stats[:, 52 + kk:53 + kk],
                          stats[:, c0:c0 + 1], stats[:, c0:c0 + 1], ALU.mult))
                S.bar(v.tensor_tensor(stats[:, 52:53], stats[:, 52:53], stats[:, 53:54], ALU.add))
                S.bar(v.tensor_tensor(stats[:, 54:55], stats[:, 54:55], stats[:, 55:56], ALU.add))
                S.bar(v.tensor_tensor(tmp8[:, 0:1], stats[:, 52:53], stats[:, 54:55], ALU.add))
                S.bar(v.scalar_tensor_tensor(var, tmp8[:, 0:1], 256.0, var, ALU.mult, ALU.add))
                S.bar(v.tensor_scalar(var, var, 1.0 / 1024.0, None, ALU.mult))
                S.bar(v.tensor_tensor(tmp8[:, 0:1], mean, mean, ALU.mult))
                return v.tensor_tensor(var, var, tmp8[:, 0:1], ALU.subtract)
            A("VE", l2stats, wait=(f"resid{ci}",), tag=f"l2s{ci}")

            def l2sqrt(sc, ci=ci):
                return sc.activation(stat2[:, 4:5], stat2[:, 4:5], ACT.Sqrt, bias=epsv[:])
            A("SC", l2sqrt, wait=(f"l2s{ci}",), tag=f"l2q{ci}")

            def l2r(v, ci=ci):
                S.bar(v.reciprocal(stat2[:, 8:9], stat2[:, 4:5]))
                S.bar(v.tensor_tensor(stat2[:, 12:13], stat2[:, 0:1], stat2[:, 8:9], ALU.mult))
                return v.tensor_scalar(stat2[:, 12:13], stat2[:, 12:13], -1.0, None, ALU.mult)
            A("VE", l2r, wait=(f"l2q{ci}",), tag=f"l2r{ci}")

            def l2z(sc, ci=ci, b=b):
                return sc.activation(zbuf[b][:], x1b[b]()[:, 0:1024], ACT.Identity,
                                     scale=stat2[:, 8:9], bias=stat2[:, 12:13])
            A("SC", l2z, wait=(f"l2r{ci}",) + ((f"h2tro{ci-2}",) if ci >= 2 else ()),
              tag=f"l2z{ci}")

            def h2tre(t, ci=ci, b=b):
                for fc in (0, 2, 4, 6):
                    inst = t.transpose(ps2b[:, (fc // 2) * 128:(fc // 2) * 128 + 128],
                                       zbuf[b][:, fc * 128:(fc + 1) * 128], identbf[:])
                return inst
            A("TE", h2tre, wait=(f"l2z{ci}", pfree["ps2"]), tag=f"h2tre{ci}")
            pfree["ps2"] = f"h2eve{ci}"

            def h2tro(t, ci=ci, b=b):
                for fc in (1, 3, 5, 7):
                    inst = t.transpose(ps3b[:, (fc // 2) * 128:(fc // 2) * 128 + 128],
                                       zbuf[b][:, fc * 128:(fc + 1) * 128], identbf[:])
                return inst
            A("TE", h2tro, wait=(f"l2z{ci}", pfree["ps3"]), tag=f"h2tro{ci}")
            pfree["ps3"] = f"h2evo{ci}"

            def h2eve(sc, ci=ci, b=b):
                for fc in (0, 2, 4, 6):
                    inst = sc.activation(h2b[b][:, fc * 128:(fc + 1) * 128],
                                         ps2b[:, (fc // 2) * 128:(fc // 2) * 128 + 128],
                                         ACT.Identity, bias=sbv[:, fc:fc + 1],
                                         scale=oncb[:, fc:fc + 1])
                return inst
            A("SC", h2eve, wait=(f"h2tre{ci}",) + ((f"h2rdfree{ci-2}",) if ci >= 2 else ()),
              tag=f"h2eve{ci}")

            def h2evo(sc, ci=ci, b=b):
                for fc in (1, 3, 5, 7):
                    inst = sc.activation(h2b[b][:, fc * 128:(fc + 1) * 128],
                                         ps3b[:, (fc // 2) * 128:(fc // 2) * 128 + 128],
                                         ACT.Identity, bias=sbv[:, fc:fc + 1],
                                         scale=oncb[:, fc:fc + 1])
                return inst
            A("SC", h2evo, wait=(f"h2tro{ci}",), tag=f"h2evo{ci}")
            S.alias(f"h2rdfree{ci}", f"h2st{ci}") if False else None

            def h2max(v, ci=ci, b=b):
                return v.tensor_reduce(mxb[:, ci:ci + 1], h2b[b], AX, ALU.max,
                                       apply_absolute_value=True)
            A("VE", h2max, wait=(f"h2eve{ci}", f"h2evo{ci}"), tag=f"h2max{ci}")

            @dma(1)
            def h2st(sc, ci=ci, b=b):
                return sc.dma_start(out=h2sp_h[ci], in_=h2b[b])
            A("SC", h2st, wait=(f"h2eve{ci}", f"h2evo{ci}"), tag=f"h2st{ci}", kind="d",
              dsem=f"HS{ci % 2}")
            S.alias(f"h2rdfree{ci}", f"h2st{ci}")

        if DBG < 5:
            if DUMP:
                emit_dumps()
            snap2 = dict(S.cnt)

            def _dfin(gp, snap2=snap2):
                for kk in tuple(S.dsems):
                    if snap2.get(kk, 0):
                        gp.wait_ge(S.sems[kk], snap2.get(kk, 0))
                return gp.engine_nop()
            A("GP", _dfin)
            S.emit(E(nc.Block()))
            return nc

        def mx1fold(v):
            return v.tensor_reduce(col1[:], mxb[:], AX, ALU.max)
        A("VE", mx1fold, wait=(f"h2max{NCH-1}",), tag="mx1fold")

        def mx1tr(t):
            return t.transpose(ps0[0:1, 0:128], col1[:], ident32[:])
        A("TE", mx1tr, wait=("mx1fold", pfree["ps0"]), tag="mx1tr")
        pfree["ps0"] = "mx1"

        def mx1(v):
            return v.tensor_reduce(scal[0:1, 0:1], ps0[0:1, 0:128], AX, ALU.max)
        A("VE", mx1, wait=("mx1tr",), tag="mx1")

        @dma(1)
        def cc1w(gp):
            return gp.dma_start(out=ccin_h, in_=scal[:])
        A("GP", cc1w, wait=("mx1",), tag="cc1w", kind="d")

        def cc1(gp):
            return gp.collective_compute("AllReduce", ALU.max,
                                         replica_groups=[list(range(NCORE))],
                                         ins=[ccin_h.opt()], outs=[ccout_h.opt()])
        A("GP", cc1, wait=("cc1w",), tag="cc1")

        @dma(1)
        def cc1r(gp):
            return gp.dma_start(out=scal2[:], in_=ccout_h)
        A("GP", cc1r, wait=("cc1",), tag="cc1r", kind="d")

        def sigx(sc):
            return sc.activation(scal[0:1, 1:2], clipx[:], ACT.Sigmoid)
        A("SC", sigx, wait=("ldc", "cc1w"), tag="sigx")

        def rxchain(v):
            S.bar(v.tensor_tensor(scal[0:1, 2:3], scal[0:1, 1:2], scal2[0:1, 0:1], ALU.mult))
            S.bar(v.reciprocal(scal[0:1, 3:4], scal[0:1, 2:3]))
            S.bar(v.tensor_scalar(scal[0:1, 4:5], scal[0:1, 3:4], QMAX, None, ALU.mult))
            return v.tensor_scalar(scal[0:1, 5:6], scal[0:1, 2:3], 1.0 / 16129.0,
                                   None, ALU.mult)
        A("VE", rxchain, wait=("cc1r", "sigx"), tag="rxchain")

        def rxbc(t):
            t.matmul(ps0[:, 0:1], ones1[:], scal[0:1, 4:5], start=True, stop=True)
            return t.matmul(ps0[:, 1:2], ones1[:], scal[0:1, 5:6], start=True, stop=True)
        A("TE", rxbc, wait=("rxchain", pfree["ps0"]), tag="rxbc")
        pfree["ps0"] = "rxv"

        def rxv(v):
            S.bar(v.tensor_copy(rx_vec[:], ps0[:, 0:1]))
            S.bar(v.tensor_scalar(lamg[:], mg_fm[:, 0:NIC], ps0[:, 1:2], None, ALU.mult))
            return v.tensor_scalar(lamu[:], mu_fm[:, 0:NIC], ps0[:, 1:2], None, ALU.mult)
        A("VE", rxv, wait=("rxbc", "gqdone", "gqdoneB", "uqdone", "uqdoneB"), tag="rxv")

        if DBG < 6:
            if DUMP:
                emit_dumps()
            snap2 = dict(S.cnt)

            def _dfin(gp, snap2=snap2):
                for kk in tuple(S.dsems):
                    if snap2.get(kk, 0):
                        gp.wait_ge(S.sems[kk], snap2.get(kk, 0))
                return gp.engine_nop()
            A("GP", _dfin)
            S.emit(E(nc.Block()))
            return nc

        for ci in range(NCH):
            b = ci % 2

            @dma(1)
            def h2rd(sy, ci=ci, b=b):
                return sy.dma_start(out=h2b[b], in_=h2sp_h[ci])
            A("SY", h2rd, wait=(f"h2st{NCH-1}", f"h2st{NCH-2}", "mx1")
              + ((f"ixq{ci-2}",) if ci >= 2 else ()),
              tag=f"h2rd{ci}", kind="d", dsem=f"HRD{ci % 2}")

            def ixq12(v, ci=ci, b=b):
                S.bar(v.tensor_scalar(h2b[b], h2b[b], rx_vec[:], MAGIC, ALU.mult, ALU.add))
                return v.tensor_scalar(h2b[b], h2b[b], MAGIC - 128.0, MAGIC + QMAX,
                                       ALU.max, ALU.min)
            A("VE", ixq12, wait=(f"h2rd{ci}", "rxv"), tag=f"ixq12_{ci}")

            def ixq3(sc, ci=ci, b=b):
                return sc.activation(
                    ybig.rearrange("p (fc t) -> p fc t", fc=8)[:, :, ci * 128:(ci + 1) * 128],
                    h2b[b].rearrange("p (fc j) -> p fc j", fc=8),
                    ACT.Identity, bias=negmg[:])
            A("SC", ixq3, wait=(f"ixq12_{ci}",), tag=f"ixq{ci}")

        ixdone = (f"ixq{NCH-1}",)

        for ic in range(NIC):
            b = ic % 2

            @dma(2)
            def igld(sy, ic=ic, b=b):
                return [sy.dma_start(out=igb[b]()[:, 0:1024].rearrange("p (fc n) -> p fc n", fc=8),
                                     in_=igst_h[ic].rearrange("fc p n -> p fc n")),
                        sy.dma_start(out=igb[b]()[:, 1024:2048].rearrange("p (fc n) -> p fc n", fc=8),
                                     in_=iust_h[ic].rearrange("fc p n -> p fc n"))]
            A("SY", igld, wait=("gqdone", "gqdoneB", "uqdone", "uqdoneB", f"ymm{NCH-1}_7")
              + ((f"gumm{ic-2}_{NTT-1}",) if ic >= 2 else ()), tag=f"igld{ic}",
              kind="d", dsem=f"IGL{ic % 2}")

            for tt in range(NTT):
                pg, pgn = (ps0, "ps0") if tt % 2 == 0 else (ps1, "ps1")
                pu, pun = (ps2, "ps2") if tt % 2 == 0 else (ps3, "ps3")

                def gumm(t, ic=ic, b=b, tt=tt, pg=pg, pu=pu):
                    for fc in range(8):
                        t.matmul(pg[:], igb[b]()[:, fc * 128:(fc + 1) * 128],
                                 ybig[:, fc * T + tt * 512:fc * T + tt * 512 + 512],
                                 start=(fc == 0), stop=(fc == 7))
                    for fc in range(8):
                        inst = t.matmul(pu[:], igb[b]()[:, 1024 + fc * 128:1024 + (fc + 1) * 128],
                                        ybig[:, fc * T + tt * 512:fc * T + tt * 512 + 512],
                                        start=(fc == 0), stop=(fc == 7))
                    return inst
                A("TE", gumm, wait=ixdone + (f"igld{ic}", pfree[pgn], pfree[pun]),
                  tag=f"gumm{ic}_{tt}")
                pfree[pgn] = f"silu{ic}_{tt}"
                pfree[pun] = f"tmul{ic}_{tt}"

                def siluf(sc, ic=ic, tt=tt, pg=pg):
                    return sc.activation(sgl[tt % 2]()[:], pg[:], ACT.Silu,
                                         scale=lamg[:, ic:ic + 1])
                A("SC", siluf, wait=(f"gumm{ic}_{tt}", "rxv")
                  + ((f"tmul{ic-1}_{tt}",) if ic > 0 else ()), tag=f"silu{ic}_{tt}")

                def tmul(v, ic=ic, tt=tt, pu=pu):
                    return v.scalar_tensor_tensor(tst[tt % 2]()[:], pu[:],
                                                  lamu[:, ic:ic + 1], sgl[tt % 2]()[:],
                                                  ALU.mult, ALU.mult)
                A("VE", tmul, wait=(f"silu{ic}_{tt}",)
                  + ((f"tst{ic-1}_{tt}", f"tmax{ic-1}_{tt}") if ic > 0 else ()),
                  tag=f"tmul{ic}_{tt}")

                def tmaxf(v, ic=ic, tt=tt):
                    return v.tensor_reduce(mxd[:, ic * NTT + tt:ic * NTT + tt + 1],
                                           tst[tt % 2]()[:], AX, ALU.max,
                                           apply_absolute_value=True)
                A("VE", tmaxf, wait=(f"tmul{ic}_{tt}",), tag=f"tmax{ic}_{tt}")

                @dma(1)
                def tstw(sc, ic=ic, tt=tt):
                    return sc.dma_start(
                        out=tsp_h[4 * tt:4 * tt + 4, ic * 128:(ic + 1) * 128, :]
                        .rearrange("c i j -> i c j"),
                        in_=tst[tt % 2]().rearrange("i (c j) -> i c j", c=4))
                A("SC", tstw, wait=(f"tmul{ic}_{tt}",), tag=f"tst{ic}_{tt}", kind="d",
                  dsem=f"TSW{tt}")

        if DBG < 7:
            if DUMP:
                emit_dumps()
            snap2 = dict(S.cnt)

            def _dfin(gp, snap2=snap2):
                for kk in tuple(S.dsems):
                    if snap2.get(kk, 0):
                        gp.wait_ge(S.sems[kk], snap2.get(kk, 0))
                return gp.engine_nop()
            A("GP", _dfin)
            S.emit(E(nc.Block()))
            return nc

        def mx2fold(v):
            return v.tensor_reduce(col1[:], mxd[:, 0:NIC * NTT], AX, ALU.max)
        A("VE", mx2fold, wait=tuple(f"tmax{NIC-1}_{tt}" for tt in range(NTT))
          + tuple(f"tmax{ic}_{NTT-1}" for ic in range(NIC)), tag="mx2fold")

        def mx2tr(t):
            return t.transpose(ps4[0:1, 0:128], col1[:], ident32[:])
        A("TE", mx2tr, wait=("mx2fold", pfree["ps4"]), tag="mx2tr")
        pfree["ps4"] = "mx2"

        def mx2(v):
            return v.tensor_reduce(scal2[0:1, 8:9], ps4[0:1, 0:128], AX, ALU.max)
        A("VE", mx2, wait=("mx2tr",), tag="mx2")

        @dma(1)
        def cc2w(gp):
            return gp.dma_start(out=ccin_h, in_=scal2[:])
        A("GP", cc2w, wait=("mx2",), tag="cc2w", kind="d")

        def cc2(gp):
            return gp.collective_compute("AllReduce", ALU.max,
                                         replica_groups=[list(range(NCORE))],
                                         ins=[ccin_h.opt()], outs=[ccout_h.opt()])
        A("GP", cc2, wait=("cc2w",), tag="cc2")

        @dma(1)
        def cc2r(gp):
            return gp.dma_start(out=scal2[:], in_=ccout_h)
        A("GP", cc2r, wait=("cc2",), tag="cc2r", kind="d")

        def sigd(sc):
            return sc.activation(scal[0:1, 9:10], clipd[:], ACT.Sigmoid)
        A("SC", sigd, wait=("ldc", "cc2w"), tag="sigd")

        def rdchain(v):
            S.bar(v.tensor_tensor(scal[0:1, 10:11], scal[0:1, 9:10], scal2[0:1, 8:9], ALU.mult))
            S.bar(v.reciprocal(scal[0:1, 11:12], scal[0:1, 10:11]))
            S.bar(v.tensor_scalar(scal[0:1, 12:13], scal[0:1, 11:12], QMAX, None, ALU.mult))
            return v.tensor_scalar(scal[0:1, 13:14], scal[0:1, 10:11], 1.0 / 16129.0,
                                   None, ALU.mult)
        A("VE", rdchain, wait=("cc2r", "sigd"), tag="rdchain")

        def rdbc(t):
            t.matmul(ps4[:, 0:1], ones1[:], scal[0:1, 12:13], start=True, stop=True)
            return t.matmul(ps4[:, 1:2], ones1[:], scal[0:1, 13:14], start=True, stop=True)
        A("TE", rdbc, wait=("rdchain", pfree["ps4"]), tag="rdbc")
        pfree["ps4"] = "rdv"

        def rdv(v):
            S.bar(v.tensor_copy(rd_vec[:], ps4[:, 0:1]))
            S.bar(v.tensor_tensor(lamfm[:], gb_v[:], md_fm[:, 0:8], ALU.mult))
            return v.tensor_scalar(lamfm[:], lamfm[:], ps4[:, 1:2], None, ALU.mult)
        A("VE", rdv, wait=("rdbc", "dqdone", "dqdoneB"), tag="rdv")

        def lamtr(t):
            return t.transpose(ps4[0:8, 0:128], lamfm[:], ident32[:])
        A("TE", lamtr, wait=("rdv",), tag="lamtr")
        pfree["ps4"] = "lam8"

        def lam8(v):
            return v.tensor_copy(r22[0:8, :], ps4[0:8, 0:128])
        A("VE", lam8, wait=("lamtr", "drrow"), tag="lam8")

        @dma(1)
        def lamrow_w(gp):
            return gp.dma_start(out=rowb_h[0:1, 0:1024], in_=r22[0:8, :])
        A("GP", lamrow_w, wait=("lam8", "drrow"), tag="lamroww", kind="d", dsem="BNC")

        @dma(1)
        def lamrow_d(gp):
            return gp.dma_start(out=lamrow, in_=rowb_h[0:1, 0:1024])
        A("GP", lamrow_d, wait=("lamroww",), tag="lamrow", kind="d", dsem="BNC")

        def lambcm(t):
            t.matmul(ps4[:, 0:512], ones1[:], lamrow[0:1, 0:512], start=True, stop=True)
            return t.matmul(ps5[:, 0:512], ones1[:], lamrow[0:1, 512:1024], start=True, stop=True)
        A("TE", lambcm, wait=("lamrow", pfree["ps4"], pfree["ps5"]), tag="lambcm")
        pfree["ps4"] = "lambc"
        pfree["ps5"] = "lambc"

        def lambc(v):
            v.tensor_copy(lam_bc[:, 0:512], ps4[:])
            return v.tensor_copy(lam_bc[:, 512:1024], ps5[:])
        A("VE", lambc, wait=("lambcm",), tag="lambc")

        @dma(2)
        def idld(sy):
            return [sy.dma_start(out=idres()[:, 0:11 * 1024].rearrange("p (ic o) -> p ic o", ic=11),
                                 in_=idst_h[0:11].rearrange("ic p o -> p ic o")),
                    sy.dma_start(out=idres()[:, 11 * 1024:NIC * 1024].rearrange("p (ic o) -> p ic o", ic=11),
                                 in_=idst_h[11:NIC].rearrange("ic p o -> p ic o"))]
        A("SY", idld, wait=("dqdone", "dqdoneB", f"gumm{NIC-1}_{NTT-1}"), tag="idld", kind="d",
          dsem="IDL")

        if DBG < 8:
            if DUMP:
                emit_dumps()
            snap2 = dict(S.cnt)

            def _dfin(gp, snap2=snap2):
                for kk in tuple(S.dsems):
                    if snap2.get(kk, 0):
                        gp.wait_ge(S.sems[kk], snap2.get(kk, 0))
                return gp.engine_nop()
            A("GP", _dfin)
            S.emit(E(nc.Block()))
            return nc

        for ci in range(NCH):
            b = ci % 2

            for hf in range(2):
                @dma(1)
                def tccld(sy, ci=ci, hf=hf):
                    return sy.dma_start(
                        out=tcc[hf]().rearrange("p (ic j) -> p ic j", ic=11),
                        in_=tsp_h[ci, hf * 1408:(hf + 1) * 1408, :]
                        .rearrange("(ic p) j -> p ic j", p=128))
                A("SY" if hf == 0 else "SC", tccld, wait=(f"tst{NIC-1}_{ci//4}",)
                  + ((f"ihq{ci-1}_{hf}",) if ci >= 1 else ()),
                  tag=f"tcc{ci}_{hf}", kind="d", dsem=f"TCC{hf}")

                def ihq12(v, ci=ci, hf=hf):
                    S.bar(v.tensor_scalar(tcc[hf]()[:], tcc[hf]()[:], rd_vec[:], MAGIC,
                          ALU.mult, ALU.add))
                    return v.tensor_scalar(tcc[hf]()[:], tcc[hf]()[:], MAGIC - 128.0,
                                           MAGIC + QMAX, ALU.max, ALU.min)
                A("VE", ihq12, wait=(f"tcc{ci}_{hf}", "rdv"), tag=f"ihq12_{ci}_{hf}")

                def ihq3(sc, ci=ci, hf=hf, b=b):
                    return sc.activation(ihc[b][:, hf * 1408:(hf + 1) * 1408],
                                         tcc[hf]()[:], ACT.Identity, bias=negmg[:])
                A("SC", ihq3, wait=(f"ihq12_{ci}_{hf}",)
                  + ((f"dmm{ci-2}",) if ci >= 2 else ()), tag=f"ihq{ci}_{hf}")

            @dma(1)
            def x1ld(sy, ci=ci, b=b):
                return sy.dma_start(out=x1b[b]()[:, 0:1024], in_=x1sp_h[ci])
            A("SY", x1ld, wait=(f"x1st{NCH-1}", f"x1st{NCH-2}")
              + ((f"outev{ci-2}",) if ci >= 2 else ()), tag=f"x1ld{ci}", kind="d",
              dsem=f"X1L{ci % 2}")

            pd0, pd0n = (ps0, "ps0") if ci % 2 == 0 else (ps2, "ps2")
            pd1, pd1n = (ps1, "ps1") if ci % 2 == 0 else (ps3, "ps3")

            def dmm(t, ci=ci, b=b, pd0=pd0, pd1=pd1):
                for ic in range(NIC):
                    t.matmul(pd0[:], ihc[b][:, ic * 128:(ic + 1) * 128],
                             idres()[:, ic * 1024:ic * 1024 + 512],
                             start=(ic == 0), stop=(ic == NIC - 1))
                for ic in range(NIC):
                    inst = t.matmul(pd1[:], ihc[b][:, ic * 128:(ic + 1) * 128],
                                    idres()[:, ic * 1024 + 512:ic * 1024 + 1024],
                                    start=(ic == 0), stop=(ic == NIC - 1))
                return inst
            A("TE", dmm, wait=(f"ihq{ci}_0", f"ihq{ci}_1", "idld",
                               pfree[pd0n], pfree[pd1n]), tag=f"dmm{ci}")
            pfree[pd0n] = f"outevv{ci}"
            pfree[pd1n] = f"outevv{ci}"

            def outevv(v, ci=ci, b=b, pd0=pd0, pd1=pd1):
                v.tensor_tensor(outb[b]()[:, 0:512], pd0[:], lam_bc[:, 0:512], ALU.mult)
                return v.tensor_tensor(outb[b]()[:, 512:1024], pd1[:], lam_bc[:, 512:1024], ALU.mult)
            A("VE", outevv, wait=(f"dmm{ci}", "lambc")
              + ((f"outst{ci-2}",) if ci >= 2 else ()), tag=f"outevv{ci}")

            def outev(gp, ci=ci, b=b):
                return gp.tensor_tensor(outb[b]()[:, 0:1024], outb[b]()[:, 0:1024],
                                        x1b[b]()[:, 0:1024], ALU.add)
            A("GP", outev, wait=(f"outevv{ci}", f"x1ld{ci}"), tag=f"outev{ci}")

            @dma(1)
            def outst(sy, ci=ci, b=b):
                return sy.dma_start(out=out_h[ci * 128:(ci + 1) * 128, :],
                                    in_=outb[b]()[:, 0:1024])
            A("SY", outst, wait=(f"outev{ci}",), tag=f"outst{ci}", kind="d",
              dsem=f"OST{ci % 2}")

        if DUMP:
            emit_dumps()

        snapf = dict(S.cnt)

        def fin(gp, snapf=snapf):
            for kk in tuple(S.dsems):
                if snapf.get(kk, 0):
                    gp.wait_ge(S.sems[kk], snapf.get(kk, 0))
            return gp.engine_nop()
        A("GP", fin, wait=(f"outst{NCH-1}",), tag="fin")

        S.emit(E(nc.Block()))
    return nc


def shard_inputs(inputs, core):
    m = dict(
        x=np.ascontiguousarray(np.asarray(inputs["L0_attn_x"])[core]),
        wqT=np.ascontiguousarray(np.asarray(inputs["L0_attn_W_q"]).T),
        wkT=np.ascontiguousarray(np.asarray(inputs["L0_attn_W_k"]).T),
        wvT=np.ascontiguousarray(np.asarray(inputs["L0_attn_W_v"]).T),
        woT=np.ascontiguousarray(np.asarray(inputs["L0_attn_W_o"]).T),
        gnw=np.ascontiguousarray(np.asarray(inputs["L0_attn_gn_weight"])),
        gnb=np.ascontiguousarray(np.asarray(inputs["L0_attn_gn_bias"])),
        clipx=np.asarray(inputs["L0_ffn_clip_x"], np.float32).reshape(1, 1),
        clipd=np.asarray(inputs["L0_ffn_clip_down"], np.float32).reshape(1, 1),
        wg=np.ascontiguousarray(np.asarray(inputs["L0_ffn_w_gate"])),
        wu=np.ascontiguousarray(np.asarray(inputs["L0_ffn_w_up"])),
        wd=np.ascontiguousarray(np.asarray(inputs["L0_ffn_w_down"])),
        wgT=np.ascontiguousarray(np.asarray(inputs["L0_ffn_w_gate"]).T),
        wuT=np.ascontiguousarray(np.asarray(inputs["L0_ffn_w_up"]).T),
        wdT=np.ascontiguousarray(np.asarray(inputs["L0_ffn_w_down"]).T),
        cvec=np.ascontiguousarray(np.asarray(inputs["L0_c"])[core:core + 1]),
        wmod=np.ascontiguousarray(np.asarray(inputs["L0_wmod"])),
    )
    m.update(host_constants())
    return {k: np.asarray(v) for k, v in m.items()}


def kernel(**inputs):
    T = np.asarray(inputs["L0_attn_x"]).shape[1]
    nc = build_program(T)
    in_maps = [shard_inputs(inputs, c) for c in range(NCORE)]
    res = run_bass_kernel_spmd(nc, in_maps, core_ids=list(range(NCORE)))
    return np.stack([np.asarray(r["out"], np.float32) for r in res.results], axis=0)


# version 09
# speedup vs baseline: 1.0258x; 1.0258x over previous
import os
import numpy as np
import ml_dtypes
import concourse.bass as bass
import concourse.mybir as mybir
from concourse.bass_utils import run_bass_kernel_spmd

F32 = mybir.dt.float32
BF16 = mybir.dt.bfloat16
AX = mybir.AxisListType.X
ALU = mybir.AluOpType
ACT = mybir.ActivationFunctionType

MAGIC = 12582912.0
D = 1024
H = 16
I = 2816
NIC = I // 128
QMAX = 127.0
LN_EPS = 1e-5
GN_EPS = 1e-5
NCORE = 8


def host_constants():
    gam = np.array([1.0 - 2.0 ** (-5.0 - i) for i in range(H)], dtype=np.float64)
    j = np.arange(128, dtype=np.float64)
    c = {}
    c["ident32"] = np.eye(128, dtype=np.float32)
    c["identbf"] = np.eye(128, dtype=ml_dtypes.bfloat16)
    c["ones1"] = np.ones((1, 128), dtype=np.float32)
    cz = (j[None, :] >= j[:, None]).astype(np.float32)
    c["causal2"] = np.concatenate([cz, cz], 1).astype(ml_dtypes.bfloat16)
    qs = np.zeros((128, 1024), np.float32)
    ks = np.zeros((128, 1024), np.float32)
    for oc in range(8):
        for half, h in ((0, 2 * oc), (64, 2 * oc + 1)):
            qs[half:half + 64, oc * 128:(oc + 1) * 128] = (gam[h] ** j / 8.0)[None, :]
            ks[half:half + 64, oc * 128:(oc + 1) * 128] = (gam[h] ** (-j))[None, :]
    c["qscale"] = qs
    c["kscale"] = ks
    gc = np.zeros((128, D), np.float32)
    for m in range(8):
        gc[0:64, 64 * m:64 * m + 64] = gam[2 * m] ** 128
        gc[64:128, 512 + 64 * m:512 + 64 * m + 64] = gam[2 * m + 1] ** 128
    c["gcscale"] = gc
    sel_e = np.zeros((8, 1024), np.float32)
    sel_o = np.zeros((8, 1024), np.float32)
    for oc in range(8):
        sel_e[oc, oc * 128:oc * 128 + 64] = 1.0
        sel_o[oc, oc * 128 + 64:oc * 128 + 128] = 1.0
    c["sel_e"] = sel_e
    c["sel_o"] = sel_o
    return c


class Sched:
    ENGS = ("SY", "SC", "VE", "GP", "TE")

    def __init__(self, nc, E):
        self.nc = nc
        self.E = E
        self.q = {e: [] for e in self.ENGS}
        self.sems = {}
        for e in self.ENGS:
            self.sems[e + "c"] = E(nc.semaphore("sem_" + e + "c"))
            self.sems[e + "x"] = E(nc.semaphore("sem_" + e + "x"))
            if e in ("SY", "SC", "GP"):
                self.sems[e + "d"] = E(nc.semaphore("sem_" + e + "d"))
        self.cnt = {k: 0 for k in self.sems}
        self.dsems = ["SYd", "SCd", "GPd"]
        self.ev = {}
        self._cur_eng = None
        self._cur_obj = None

    def bar(self, inst):
        e = self._cur_eng
        sem = self.sems[e + "x"]
        self.cnt[e + "x"] += 1
        inst.then_inc(sem, 1)
        self._cur_obj.wait_ge(sem, self.cnt[e + "x"])
        return inst

    def add(self, eng, fn, wait=(), tag=None, kind="c", dsem=None):
        for w in wait:
            if w is not None and w not in self.ev:
                raise KeyError(f"event {w!r} undefined (tag={tag} eng={eng})")
        if dsem is not None:
            assert kind == "d"
            if dsem not in self.sems:
                self.sems[dsem] = self.E(self.nc.semaphore("sem_" + dsem))
                self.cnt[dsem] = 0
                self.dsems.append(dsem)
            key = dsem
        else:
            key = eng + kind
        self.q[eng].append((tuple(w for w in wait if w is not None), fn, tag, kind, key))
        if kind == "d":
            self.cnt[key] += 16 * getattr(fn, "ndma", 1)
        else:
            self.cnt[key] += 1
        if tag is not None:
            self.ev[tag] = (self.sems[key], self.cnt[key])

    def alias(self, new, old):
        self.ev[new] = self.ev[old]

    def emit(self, block):
        secs = {"SY": block.sync, "SC": block.scalar, "VE": block.vector,
                "GP": block.gpsimd, "TE": block.tensor}
        for e in self.ENGS:
            items = self.q[e]
            if not items:
                continue

            def body(eng, e=e, items=items):
                self._cur_eng = e
                self._cur_obj = eng
                for waitl, fn, tag, kind, key in items:
                    best = {}
                    for w in waitl:
                        sem, val = self.ev[w]
                        kk = sem.name
                        if best.get(kk, (None, -1))[1] < val:
                            best[kk] = (sem, val)
                    for sem, val in best.values():
                        eng.wait_ge(sem, val)
                    sem = self.sems[key]
                    if kind == "d":
                        insts = fn(eng)
                        if not isinstance(insts, (list, tuple)):
                            insts = [insts]
                        n = getattr(fn, "ndma", 1)
                        assert len(insts) == n, f"{tag}: ndma {n} != {len(insts)}"
                        for inst in insts:
                            inst.then_inc(sem, 16)
                    else:
                        fn(eng).then_inc(sem, 1)
            secs[e](body)


def dma(n):
    def wrap(fn):
        fn.ndma = n
        return fn
    return wrap


def build_program(T):
    DBG = int(os.environ.get("KDBG", "99"))
    DUMP = int(os.environ.get("KDUMP", "0"))
    PRO = int(os.environ.get("KPRO", "99"))
    PA = int(os.environ.get("KPA", "99"))
    NG = T // 512
    NCH = T // 128
    NTT = T // 512
    nc = bass.Bass("TRN2", target_bir_lowering=False, debug=False,
                   detect_race_conditions=False)

    def din(name, shape, dt=F32):
        return nc.dram_tensor(name, shape, dt, kind="ExternalInput").ap()

    x_h = din("x", [T, D])
    wqT_h = din("wqT", [D, D]); wkT_h = din("wkT", [D, D])
    wvT_h = din("wvT", [D, D]); woT_h = din("woT", [D, D])
    gnw_h = din("gnw", [D]); gnb_h = din("gnb", [D])
    clipx_h = din("clipx", [1, 1]); clipd_h = din("clipd", [1, 1])
    wg_h = din("wg", [I, D]); wu_h = din("wu", [I, D]); wd_h = din("wd", [D, I])
    wgT_h = din("wgT", [D, I]); wuT_h = din("wuT", [D, I]); wdT_h = din("wdT", [I, D])
    cvec_h = din("cvec", [1, D])
    wmod_h = din("wmod", [D, 6 * D])
    CONSTS = host_constants()
    ch = {k: din(k, list(v.shape), BF16 if v.dtype == ml_dtypes.bfloat16 else F32)
          for k, v in CONSTS.items()}

    out_h = nc.dram_tensor("out", [T, D], F32, kind="ExternalOutput").ap()

    wbf_h = nc.dram_tensor("wbf", [4, D, D], BF16).ap()
    igst_h = nc.dram_tensor("igst", [NIC, 8, 128, 128], BF16).ap()
    iust_h = nc.dram_tensor("iust", [NIC, 8, 128, 128], BF16).ap()
    idst_h = nc.dram_tensor("idst", [NIC, 128, D], BF16).ap()
    h2sp_h = nc.dram_tensor("h2sp", [NCH, 128, D], F32).ap()
    x1sp_h = nc.dram_tensor("x1sp", [NCH, 128, D], F32).ap()
    tsp_h = nc.dram_tensor("tsp", [NCH, I, 128], F32).ap()
    rowb_h = nc.dram_tensor("rowb", [1, 2816], F32).ap()
    ccin_h = nc.dram_tensor("ccin", [1, 16], F32).ap()
    ccout_h = nc.dram_tensor("ccout", [1, 16], F32).ap()

    from contextlib import ExitStack
    with ExitStack() as stack:
        E = stack.enter_context
        sbt = lambda name, shape, dt=F32: E(nc.sbuf_tensor("sb_" + name, shape, dt))

        ident32 = sbt("ident32", [128, 128])
        identbf = sbt("identbf", [128, 128], BF16)
        ones1 = sbt("ones1", [1, 128])
        causal2 = sbt("causal2", [128, 256], BF16)
        qscale = sbt("qscale", [128, 1024])
        kscale = sbt("kscale", [128, 1024])
        gcscale = sbt("gcscale", [128, 1024])
        sel_e = sbt("sel_e", [8, 1024]); sel_o = sbt("sel_o", [8, 1024])
        gnwv = sbt("gnwv", [128, 8]); gnbv = sbt("gnbv", [128, 8])
        clipx = sbt("clipx", [1, 1]); clipd = sbt("clipd", [1, 1])
        cfm = sbt("cfm", [128, 8]); silc = sbt("silc", [128, 8], BF16)
        modfm = sbt("modfm", [128, 48])
        onca = sbt("onca", [128, 8]); sa_v = sbt("sa_v", [128, 8])
        oncb = sbt("oncb", [128, 8]); sbv = sbt("sbv", [128, 8])
        ga_v = sbt("ga_v", [128, 8]); gb_v = sbt("gb_v", [128, 8])
        ga_bc = sbt("ga_bc", [128, 1024])
        mg_fm = sbt("mg_fm", [128, NIC]); mu_fm = sbt("mu_fm", [128, NIC])
        md_fm = sbt("md_fm", [128, 8])
        lamg = sbt("lamg", [128, NIC]); lamu = sbt("lamu", [128, NIC])
        rx_vec = sbt("rx_vec", [128, 1]); rd_vec = sbt("rd_vec", [128, 1])
        scal = sbt("scal", [1, 16]); scal2 = sbt("scal2", [1, 16])
        stats = sbt("stats", [128, 64])
        stat2 = sbt("stat2", [128, 16])
        tmp8 = sbt("tmp8", [128, 16])
        gnsy = sbt("gnsy", [128, 8]); gnsyy = sbt("gnsyy", [128, 8])
        gnst = sbt("gnst", [128, 48])
        hse = sbt("hse", [8, 8]); hso = sbt("hso", [8, 8])
        gnA = sbt("gnA", [128, 8]); gnB = sbt("gnB", [128, 8])
        lamfm = sbt("lamfm", [128, 8])
        mxb = sbt("mxb", [128, NCH]); mxd = sbt("mxd", [128, NIC * NTT])
        col1 = sbt("col1", [128, 1])
        epsv = sbt("epsv", [128, 1])
        negmg = sbt("negmg", [128, 1])
        r22 = sbt("r22", [22, 128])
        S_all = [sbt("S_a", [128, 1024], BF16), sbt("S_b", [128, 1024], BF16)]
        ar3 = sbt("ar3", [128, 4096])
        ar3b = ar3.bitcast(BF16)
        xbuf = [ar3[:, 0:1024], ar3[:, 1024:2048], ar3[:, 2048:3072], ar3[:, 3072:4096]]
        lam_bc = ar3[:, 2816:3840]
        ihc = [ar3b[:, 0:2816], ar3b[:, 2816:5632]]
        zbuf = [sbt("zb0", [128, 1024], BF16), sbt("zb1", [128, 1024], BF16)]
        ktb = [sbt("ktb0", [128, 1024], BF16), sbt("ktb1", [128, 1024], BF16)]
        vtb2 = [sbt("vtb0", [128, 1024], BF16), sbt("vtb1", [128, 1024], BF16)]
        amb = [sbt("amb0", [128, 256], BF16), sbt("amb1", [128, 256], BF16)]
        ybig = sbt("ybig", [128, 8 * T], BF16)
        ar1 = sbt("ar1", [128, 11264])
        ar2 = sbt("ar2", [128, 9088])
        ar1b = ar1.bitcast(BF16)
        ar2b = ar2.bitcast(BF16)

        vtb = vtb2 + [ar1b[:, 12288:13312], ar1b[:, 13312:14336]]
        wtk = [ar1b[:, 14336:15360], ar1b[:, 15360:16384]]
        wtq = [ar1b[:, 20480:21504], ar1b[:, 21504:22528]]
        h2b = [ar1[:, 4096:5120], ar1[:, 5120:6144]]
        ynb = ar2b[:, 4096:5120]
        ynbs = [ar2b[:, 4096:5120], ar2b[:, 5120:6144]]
        qsb = lambda: ar1b[:, 0:4096]
        ksb = lambda: ar1b[:, 4096:8192]
        hbf = lambda: ar1b[:, 8192:12288]
        wkw = lambda: ar1b[:, 12288:16384]
        wvw = lambda: ar1b[:, 16384:20480]
        wmt = [lambda: ar1b[:, 0:8192], lambda: ar1b[:, 8192:16384]]
        igb = [lambda: ar1b[:, 0:2048], lambda: ar1b[:, 2048:4096]]
        idres = lambda: ar1b[:, 0:NIC * 1024]
        wna = [lambda: ar2[:, 0:1024], lambda: ar2[:, 1024:2048]]
        wqa = [lambda: ar2[:, 2048:3456], lambda: ar2[:, 3456:4864]]
        iqa = [lambda: ar2b[:, 9728:11136], lambda: ar2b[:, 11136:12544]]
        rbc = lambda: ar2[:, 6272:9088]
        x1b = [lambda: ar2[:, 0:1024], lambda: ar2[:, 1024:2048]]
        tst = [lambda: ar2[:, 2048:2560], lambda: ar2[:, 2560:3072]]
        sgl = [lambda: ar2[:, 3072:3584], lambda: ar2[:, 3584:4096]]
        tcc = [lambda: ar2[:, 4096:5504], lambda: ar2[:, 5504:6912]]
        outb = [lambda: ar2[:, 6912:7936], lambda: ar2[:, 7936:8960]]
        rrow = ar2[0:1, 2048:4864]
        garow = ar2[0:1, 6272:7296]
        lamrow = ar2[0:1, 7296:8320]
        l2tmp = ar2[:, 4096:5120]

        ps0 = E(nc.psum_tensor("ps0", [128, 512], F32))
        ps1 = E(nc.psum_tensor("ps1", [128, 512], F32))
        ps2 = E(nc.psum_tensor("ps2", [128, 512], F32))
        ps3 = E(nc.psum_tensor("ps3", [128, 512], F32))
        ps4 = E(nc.psum_tensor("ps4", [128, 512], F32))
        ps5 = E(nc.psum_tensor("ps5", [128, 512], F32))
        psS = E(nc.psum_tensor("psS", [128, 1024], F32))
        ps0b = ps0.bitcast(BF16); ps1b = ps1.bitcast(BF16)
        ps2b = ps2.bitcast(BF16); ps3b = ps3.bitcast(BF16)

        S = Sched(nc, E)
        A = S.add
        pfree = {}

        @dma(14)
        def ld_consts(sy):
            r = []
            for st, hh in ((ident32, ch["ident32"]), (identbf, ch["identbf"]),
                           (ones1, ch["ones1"]), (causal2, ch["causal2"]),
                           (qscale, ch["qscale"]), (kscale, ch["kscale"]),
                           (gcscale, ch["gcscale"]),
                           (sel_e, ch["sel_e"]), (sel_o, ch["sel_o"]),
                           (clipx, clipx_h), (clipd, clipd_h)):
                r.append(sy.dma_start(out=st[:], in_=hh))
            r.append(sy.dma_start(out=gnwv[:], in_=gnw_h.rearrange("(j p) -> p j", p=128),
                                  allow_slow_non_contiguous=True))
            r.append(sy.dma_start(out=gnbv[:], in_=gnb_h.rearrange("(j p) -> p j", p=128),
                                  allow_slow_non_contiguous=True))
            r.append(sy.dma_start(out=cfm[:], in_=cvec_h.rearrange("u (j p) -> p u j", p=128),
                                  allow_slow_non_contiguous=True))
            return r
        A("SY", ld_consts, tag="ldc", kind="d")
        for k in ("ps0", "ps1", "ps2", "ps3", "ps4", "ps5", "psS"):
            pfree[k] = "ldc"

        def init0(v):
            v.memset(epsv[:], LN_EPS)
            v.memset(negmg[:], -MAGIC)
            v.memset(S_all[0][:], 0.0)
            v.memset(S_all[1][:], 0.0)
            v.memset(scal[:], 0.0)
            return v.memset(scal2[:], 0.0)
        A("VE", init0, wait=("ldc",), tag="init0")

        if PRO < 1:
            def _dfin(gp):
                for kk in tuple(S.dsems):
                    if S.cnt[kk]:
                        gp.wait_ge(S.sems[kk], S.cnt[kk])
                return gp.engine_nop()
            A("GP", _dfin)
            S.emit(E(nc.Block()))
            return nc

        @dma(4)
        def wstage(gp):
            return [gp.dma_start(out=wbf_h[k], in_=wh)
                    for k, wh in enumerate((wqT_h, wkT_h, wvT_h, woT_h))]
        A("GP", wstage, tag="wstage", kind="d", dsem="WST")

        if PRO < 2:
            def _dfin(gp):
                for kk in tuple(S.dsems):
                    if S.cnt[kk]:
                        gp.wait_ge(S.sems[kk], S.cnt[kk])
                return gp.engine_nop()
            A("GP", _dfin)
            S.emit(E(nc.Block()))
            return nc

        def silu_c(sc):
            return sc.activation(silc[:], cfm[:], ACT.Silu)
        A("SC", silu_c, wait=("ldc",), tag="silc")

        for b8 in range(6):
            bb = b8 % 2

            @dma(1)
            def wmod_ld(gp, b8=b8, bb=bb):
                return gp.dma_start(
                    out=wmt[bb]().rearrange("p (fc n) -> p fc n", fc=8),
                    in_=wmod_h[:, b8 * 1024:(b8 + 1) * 1024]
                    .rearrange("(fc p) n -> p fc n", p=128))
            A("GP", wmod_ld, wait=("ldc", "wstage") + ((f"modmm{b8-2}",) if b8 >= 2 else ()),
              tag=f"wmod{b8}", kind="d", dsem=f"WMD{bb}")

            def modmm(t, b8=b8, bb=bb):
                pst = ps4 if b8 % 2 == 0 else ps5
                for lb in range(8):
                    for fc in range(8):
                        inst = t.matmul(pst[:, lb:lb + 1],
                                        wmt[bb]()[:, fc * 1024 + lb * 128:fc * 1024 + lb * 128 + 128],
                                        silc[:, fc:fc + 1],
                                        start=(fc == 0), stop=(fc == 7))
                return inst
            pn = "ps4" if b8 % 2 == 0 else "ps5"
            A("TE", modmm, wait=(f"wmod{b8}", "silc", pfree[pn]), tag=f"modmm{b8}")
            pfree[pn] = f"modev{b8}"

            def modev(v, b8=b8):
                pst = ps4 if b8 % 2 == 0 else ps5
                return v.tensor_copy(modfm[:, b8 * 8:(b8 + 1) * 8], pst[:, 0:8])
            A("VE", modev, wait=(f"modmm{b8}",), tag=f"modev{b8}")

        if PRO < 3:
            def _dfin(gp):
                for kk in tuple(S.dsems):
                    if S.cnt[kk]:
                        gp.wait_ge(S.sems[kk], S.cnt[kk])
                return gp.engine_nop()
            A("GP", _dfin)
            S.emit(E(nc.Block()))
            return nc

        def modpost(v):
            S.bar(v.tensor_copy(sa_v[:], modfm[:, 0:8]))
            S.bar(v.tensor_scalar(onca[:], modfm[:, 8:16], 1.0, None, ALU.add))
            S.bar(v.tensor_copy(sbv[:], modfm[:, 24:32]))
            S.bar(v.tensor_scalar(oncb[:], modfm[:, 32:40], 1.0, None, ALU.add))
            for src, dst in ((16, ga_v), (40, gb_v)):
                g0 = modfm[:, src:src + 8]
                S.bar(v.tensor_scalar(tmp8[:, 0:8], g0, 0.0, 2.0, ALU.is_ge, ALU.mult))
                S.bar(v.tensor_scalar(tmp8[:, 0:8], tmp8[:, 0:8], -1.0, None, ALU.add))
                S.bar(v.tensor_tensor(tmp8[:, 8:16], g0, tmp8[:, 0:8], ALU.mult))
                S.bar(v.tensor_scalar(tmp8[:, 8:16], tmp8[:, 8:16], 0.25, None, ALU.max))
                inst = v.tensor_tensor(dst[:], tmp8[:, 0:8], tmp8[:, 8:16], ALU.mult)
            return inst
        A("VE", modpost, wait=("modev5",), tag="modpost")

        if PRO < 4:
            def _dfin(gp):
                for kk in tuple(S.dsems):
                    if S.cnt[kk]:
                        gp.wait_ge(S.sems[kk], S.cnt[kk])
                return gp.engine_nop()
            A("GP", _dfin)
            S.emit(E(nc.Block()))
            return nc

        def gatr(t):
            return t.transpose(ps4[0:8, 0:128], ga_v[:], ident32[:])
        A("TE", gatr, wait=("modpost", pfree["ps4"]), tag="gatr")
        pfree["ps4"] = "gat8"

        def gat8(v):
            return v.tensor_copy(r22[0:8, :], ps4[0:8, 0:128])
        A("VE", gat8, wait=("gatr",), tag="gat8")

        if PRO < 5:
            def _dfin(gp):
                for kk in tuple(S.dsems):
                    if S.cnt[kk]:
                        gp.wait_ge(S.sems[kk], S.cnt[kk])
                return gp.engine_nop()
            A("GP", _dfin)
            S.emit(E(nc.Block()))
            return nc

        GM = int(os.environ.get("KGM", "0"))
        if GM == 3:
            S.alias("garoww", "gat8"); S.alias("garow", "gat8")
        else:
            @dma(1)
            def garow_w(gp):
                return gp.dma_start(out=rowb_h[0:1, 0:1024], in_=r22[0:8, :])
            A("GP", garow_w, wait=("gat8",), tag="garoww", kind="d", dsem="BNC")
            if GM == 2:
                S.alias("garow", "garoww")
            else:
                gdst = sbt("garow_ded", [1, 1024]).ap() if GM == 1 else garow

                @dma(1)
                def garow_d(gp, gdst=gdst):
                    return gp.dma_start(out=gdst, in_=rowb_h[0:1, 0:1024])
                A("GP", garow_d, wait=("garoww",), tag="garow", kind="d", dsem="BNC")

        if PRO < 6:
            def _dfin(gp):
                for kk in tuple(S.dsems):
                    if S.cnt[kk]:
                        gp.wait_ge(S.sems[kk], S.cnt[kk])
                return gp.engine_nop()
            A("GP", _dfin)
            S.emit(E(nc.Block()))
            return nc

        def gabcm(t):
            t.matmul(ps4[:, 0:512], ones1[:], garow[0:1, 0:512], start=True, stop=True)
            return t.matmul(ps5[:, 0:512], ones1[:], garow[0:1, 512:1024], start=True, stop=True)
        A("TE", gabcm, wait=("garow", pfree["ps4"], pfree["ps5"]), tag="gabcm")
        pfree["ps4"] = "gabc"
        pfree["ps5"] = "gabc"

        def gabc(v):
            v.tensor_copy(ga_bc[:, 0:512], ps4[:])
            return v.tensor_copy(ga_bc[:, 512:1024], ps5[:])
        A("VE", gabc, wait=("gabcm",), tag="gabc")

        if DBG < 2:
            if DUMP:
                emit_dumps()
            snap2 = dict(S.cnt)

            def _dfin(gp, snap2=snap2):
                for kk in tuple(S.dsems):
                    if snap2.get(kk, 0):
                        gp.wait_ge(S.sems[kk], snap2.get(kk, 0))
                return gp.engine_nop()
            A("GP", _dfin)
            S.emit(E(nc.Block()))
            return nc

        def emit_dumps():
            dmp = {}
            for name, shape, dt in (
                    ("d_modfm", [128, 48], F32), ("d_hbuf", [128, 4096], BF16),
                    ("d_qsb", [128, 4096], BF16), ("d_ksb", [128, 4096], BF16),
                    ("d_ktb0", [128, 1024], BF16), ("d_vtb0", [128, 1024], BF16),
                    ("d_ybig", [128, 8 * T], BF16), ("d_Sa", [128, 1024], BF16),
                    ("d_Sb", [128, 1024], BF16), ("d_zb0", [128, 1024], BF16),
                    ("d_scal", [1, 16], F32), ("d_scal2", [1, 16], F32),
                    ("d_mxd", [128, NIC * NTT], F32), ("d_col1", [128, 1], F32),
                    ("d_mxb", [128, NCH], F32)):
                dmp[name] = nc.dram_tensor(name, shape, dt, kind="ExternalOutput").ap()

            snap = dict(S.cnt)

            def drain(gp, snap=snap):
                for kk in tuple(S.dsems) + ("VEc", "SCc", "TEc", "GPc"):
                    if snap.get(kk, 0):
                        gp.wait_ge(S.sems[kk], snap.get(kk, 0))
                return gp.engine_nop()
            A("GP", drain, tag="draindump")

            @dma(15)
            def dumps(gp):
                r = []
                for ap, src in ((dmp["d_modfm"], modfm[:]), (dmp["d_hbuf"], hbf()[:]),
                                (dmp["d_qsb"], qsb()[:]), (dmp["d_ksb"], ksb()[:]),
                                (dmp["d_ktb0"], ktb[0][:]), (dmp["d_vtb0"], vtb[0][:, 0:1024]),
                                (dmp["d_ybig"], ybig[:]), (dmp["d_Sa"], S_all[0][:]),
                                (dmp["d_Sb"], S_all[1][:]), (dmp["d_zb0"], zbuf[0][:]),
                                (dmp["d_scal"], scal[:]), (dmp["d_scal2"], scal2[:]),
                                (dmp["d_mxd"], mxd[:]), (dmp["d_col1"], col1[:]),
                                (dmp["d_mxb"], mxb[:])):
                    r.append(gp.dma_start(out=ap, in_=src))
                return r
            A("GP", dumps, wait=("draindump",), tag="dumps", kind="d")

        def emit_rowmax(tag, nat_h, nrow, ncol, mfm):
            ntile = nrow // 128
            nch = (ncol + 1023) // 1024
            k = 0
            for tix in range(ntile):
                for chx in range(nch):
                    b = k % 2
                    c0 = chx * 1024
                    cw = min(1024, ncol - c0)
                    yield

                    @dma(1)
                    def natld(gp, tix=tix, b=b, nat_h=nat_h, c0=c0, cw=cw):
                        return gp.dma_start(out=wna[b]()[:, 0:cw],
                                            in_=nat_h[tix * 128:(tix + 1) * 128, c0:c0 + cw])
                    A("GP", natld, wait=("ldc",) + ((f"{tag}rm{k-2}",) if k >= 2 else ()),
                      tag=f"{tag}nat{k}", kind="d", dsem=f"{tag}NA{b}")

                    def rmax(v, tix=tix, b=b, mfm=mfm, cw=cw, chx=chx):
                        if chx == 0:
                            return v.tensor_reduce(mfm[:, tix:tix + 1], wna[b]()[:, 0:cw],
                                                   AX, ALU.max, apply_absolute_value=True)
                        S.bar(v.tensor_reduce(col1[:], wna[b]()[:, 0:cw],
                                              AX, ALU.max, apply_absolute_value=True))
                        return v.tensor_tensor(mfm[:, tix:tix + 1], mfm[:, tix:tix + 1],
                                               col1[:], ALU.max)
                    A("VE", rmax, wait=(f"{tag}nat{k}",), tag=f"{tag}rm{k}")
                    k += 1
            S.alias(f"{tag}rmdone", f"{tag}rm{k-1}")
            yield

        def emit_rbc(tag, mfm, ntile, width, prev=None):
            yield
            def rfm(v, mfm=mfm, ntile=ntile):
                S.bar(v.reciprocal(gnst[:, 0:ntile], mfm[:, 0:ntile]))
                return v.tensor_scalar(gnst[:, 0:ntile], gnst[:, 0:ntile], QMAX, None, ALU.mult)
            A("VE", rfm, wait=(f"{tag}rmdone",)
              + ((f"{prev}rtr",) if prev else ()), tag=f"{tag}rfm")

            def rtr(t, ntile=ntile):
                return t.transpose(ps4[0:ntile, 0:128], gnst[:, 0:ntile], ident32[:])
            A("TE", rtr, wait=(f"{tag}rfm", pfree["ps4"]), tag=f"{tag}rtr")
            pfree["ps4"] = f"{tag}r22"

            def r22e(v, ntile=ntile):
                return v.tensor_copy(r22[0:ntile, :], ps4[0:ntile, 0:128])
            A("VE", r22e, wait=(f"{tag}rtr",)
              + ((f"{prev}rrow",) if prev else ("garow",)), tag=f"{tag}r22")

            @dma(1)
            def rroww(gp, ntile=ntile, width=width):
                return gp.dma_start(out=rowb_h[0:1, 0:width], in_=r22[0:ntile, :])
            A("GP", rroww, wait=(f"{tag}r22", "garow"), tag=f"{tag}rroww", kind="d", dsem="BNC")

            @dma(1)
            def rrowd(gp, width=width):
                return gp.dma_start(out=rrow[0:1, 0:width], in_=rowb_h[0:1, 0:width])
            A("GP", rrowd, wait=(f"{tag}rroww",), tag=f"{tag}rrow", kind="d", dsem="BNC")

            nmm = (width + 511) // 512
            for m in range(nmm):
                c0 = m * 512
                cw = min(512, width - c0)
                pn = "ps4" if m % 2 == 0 else "ps5"
                pst = ps4 if m % 2 == 0 else ps5

                def rbcm(t, pst=pst, c0=c0, cw=cw):
                    return t.matmul(pst[:, 0:cw], ones1[:], rrow[0:1, c0:c0 + cw],
                                    start=True, stop=True)
                A("TE", rbcm, wait=(f"{tag}rrow", pfree[pn]), tag=f"{tag}rbcm{m}")
                pfree[pn] = f"{tag}rbc{m}"

                def rbce(v, pst=pst, c0=c0, cw=cw):
                    return v.tensor_copy(rbc()[:, c0:c0 + cw], pst[:, 0:cw])
                A("VE", rbce, wait=(f"{tag}rbcm{m}",), tag=f"{tag}rbc{m}")
            S.alias(f"{tag}rbc", f"{tag}rbc{nmm-1}")
            yield

        def emit_wquant(tag, wT_h, width, st_h):
            nrows = wT_h.shape[0]
            nfc = nrows // 128
            halfw = width // 2 if width > 1408 else width
            nhh = width // halfw
            nk = nfc * nhh
            for fc in range(nfc):
                for hh in range(nhh):
                    k = fc * nhh + hh
                    b = k % 2
                    c0 = hh * halfw
                    yield

                    @dma(1)
                    def wl(gp, fc=fc, c0=c0, b=b, wT_h=wT_h, halfw=halfw):
                        return gp.dma_start(out=wqa[b]()[:, 0:halfw],
                                            in_=wT_h[fc * 128:(fc + 1) * 128, c0:c0 + halfw])
                    A("GP", wl, wait=((f"{tag}q3_{k-2}",) if k >= 2 else ()) + (f"{tag}rbc",),
                      tag=f"{tag}wl{k}", kind="d", dsem=f"{tag}WL{b}")

                    def q3(v, k=k, b=b, c0=c0, halfw=halfw):
                        S.bar(v.tensor_tensor(wqa[b]()[:, 0:halfw], wqa[b]()[:, 0:halfw],
                              rbc()[:, c0:c0 + halfw], ALU.mult))
                        S.bar(v.tensor_scalar(wqa[b]()[:, 0:halfw], wqa[b]()[:, 0:halfw],
                              MAGIC, MAGIC - 128.0, ALU.add, ALU.max))
                        return v.tensor_scalar(iqa[b]()[:, 0:halfw], wqa[b]()[:, 0:halfw],
                                               MAGIC + QMAX, MAGIC, ALU.min, ALU.subtract)
                    A("VE", q3, wait=(f"{tag}wl{k}", f"{tag}rbc")
                      + ((f"{tag}st{k-2}",) if k >= 2 else ()), tag=f"{tag}q3_{k}")

                    @dma(1)
                    def stq(gp, fc=fc, hh=hh, b=b, st_h=st_h, halfw=halfw):
                        if st_h is idst_h:
                            dst = st_h[fc]
                            return gp.dma_start(out=dst, in_=iqa[b]()[:, 0:halfw])
                        nt = halfw // 128
                        i0 = hh * nt
                        return gp.dma_start(out=st_h[i0:i0 + nt, fc].rearrange("i p n -> p i n"),
                                            in_=iqa[b]()[:, 0:halfw]
                                            .rearrange("p (i n) -> p i n", n=128))
                    A("GP", stq, wait=(f"{tag}q3_{k}",), tag=f"{tag}st{k}", kind="d",
                      dsem=f"{tag}QS{b}")
            S.alias(f"{tag}qdone", f"{tag}st{nk-1}")
            S.alias(f"{tag}qdoneB", f"{tag}st{nk-2}")
            yield


        import itertools
        quant_gen = itertools.chain(
            emit_rowmax("g", wg_h, I, D, mg_fm),
            emit_rbc("g", mg_fm, NIC, I),
            emit_wquant("g", wgT_h, I, igst_h),
            emit_rowmax("u", wu_h, I, D, mu_fm),
            emit_rbc("u", mu_fm, NIC, I, prev="g"),
            emit_wquant("u", wuT_h, I, iust_h),
            emit_rowmax("d", wd_h, D, I, md_fm),
            emit_rbc("d", md_fm, 8, D, prev="u"),
            emit_wquant("d", wdT_h, D, idst_h))

        def run_quant_chunk():
            n = max(1, 210 // max(NCH, 1))
            for _ in range(n + 2):
                next(quant_gen, None)

        for g in range(NG):
            for i4 in range(4):
                tix = g * 4 + i4
                b = tix % 2

                @dma(1)
                def xld(sy, t0=g * 512 + i4 * 128, b4=tix % 4):
                    return sy.dma_start(out=xbuf[b4], in_=x_h[t0:t0 + 128, :])
                A("SY", xld, wait=((f"z{tix-4}",) if tix >= 4 else ("ldc",)),
                  tag=f"xld{tix}", kind="d", dsem=f"XB{tix % 4}")

            def stats_f(v, g=g):
                for i4 in range(4):
                    S.bar(v.bn_stats(stats[:, i4 * 12:i4 * 12 + 6], xbuf[(g * 4 + i4) % 4][:, 0:512]))
                    S.bar(v.bn_stats(stats[:, i4 * 12 + 6:i4 * 12 + 12], xbuf[(g * 4 + i4) % 4][:, 512:1024]))
                mean = stat2[:, 0:4]; var = stat2[:, 4:8]
                S.bar(v.tensor_tensor(stats[:, 48:52], stats[:, 1:48:12], stats[:, 4:48:12], ALU.add))
                S.bar(v.tensor_tensor(stats[:, 52:56], stats[:, 7:48:12], stats[:, 10:48:12], ALU.add))
                S.bar(v.tensor_tensor(mean, stats[:, 48:52], stats[:, 52:56], ALU.add))
                S.bar(v.tensor_scalar(mean, mean, 0.25, None, ALU.mult))
                S.bar(v.tensor_tensor(stats[:, 48:52], stats[:, 2:48:12], stats[:, 5:48:12], ALU.add))
                S.bar(v.tensor_tensor(stats[:, 52:56], stats[:, 8:48:12], stats[:, 11:48:12], ALU.add))
                S.bar(v.tensor_tensor(var, stats[:, 48:52], stats[:, 52:56], ALU.add))
                for kk, c0 in enumerate((1, 4, 7, 10)):
                    S.bar(v.tensor_tensor(stats[:, 48 + 4 * kk:52 + 4 * kk],
                                          stats[:, c0:48:12], stats[:, c0:48:12], ALU.mult))
                S.bar(v.tensor_tensor(stats[:, 48:52], stats[:, 48:52], stats[:, 52:56], ALU.add))
                S.bar(v.tensor_tensor(stats[:, 56:60], stats[:, 56:60], stats[:, 60:64], ALU.add))
                S.bar(v.tensor_tensor(tmp8[:, 0:4], stats[:, 48:52], stats[:, 56:60], ALU.add))
                S.bar(v.scalar_tensor_tensor(var, tmp8[:, 0:4], 256.0, var, ALU.mult, ALU.add))
                S.bar(v.tensor_scalar(var, var, 1.0 / 1024.0, None, ALU.mult))
                S.bar(v.tensor_tensor(tmp8[:, 0:4], mean, mean, ALU.mult))
                return v.tensor_tensor(var, var, tmp8[:, 0:4], ALU.subtract)
            A("VE", stats_f, wait=tuple(f"xld{g*4+i}" for i in range(4)), tag=f"l1v{g}")

            def sqrt_f(sc, g=g):
                return sc.activation(stat2[:, 4:8], stat2[:, 4:8], ACT.Sqrt, bias=epsv[:])
            A("SC", sqrt_f, wait=(f"l1v{g}",), tag=f"l1q{g}")

            def rstd_f(v, g=g):
                S.bar(v.reciprocal(stat2[:, 8:12], stat2[:, 4:8]))
                S.bar(v.tensor_tensor(stat2[:, 12:16], stat2[:, 0:4], stat2[:, 8:12], ALU.mult))
                return v.tensor_scalar(stat2[:, 12:16], stat2[:, 12:16], -1.0, None, ALU.mult)
            A("VE", rstd_f, wait=(f"l1q{g}",), tag=f"l1r{g}")

            for i4 in range(4):
                tix = g * 4 + i4
                b = tix % 2

                def z_f(sc, i4=i4, b=b, b4=tix % 4):
                    return sc.activation(zbuf[b][:], xbuf[b4], ACT.Identity,
                                         scale=stat2[:, 8 + i4:9 + i4],
                                         bias=stat2[:, 12 + i4:13 + i4])
                A("SC", z_f, wait=(f"l1r{g}",) + ((f"htro{tix-2}",) if tix >= 2 and PA >= 2 else ()),
                  tag=f"z{tix}")
                if PA < 2:
                    continue

                def htre(t, tix=tix, b=b):
                    for fc in (0, 2, 4, 6):
                        inst = t.transpose(ps0b[:, (fc // 2) * 128:(fc // 2) * 128 + 128],
                                           zbuf[b][:, fc * 128:(fc + 1) * 128], identbf[:])
                    return inst
                A("TE", htre, wait=(f"z{tix}", "modpost", pfree["ps0"]), tag=f"htre{tix}")
                pfree["ps0"] = f"heve{tix}"

                def htro(t, tix=tix, b=b):
                    for fc in (1, 3, 5, 7):
                        inst = t.transpose(ps1b[:, (fc // 2) * 128:(fc // 2) * 128 + 128],
                                           zbuf[b][:, fc * 128:(fc + 1) * 128], identbf[:])
                    return inst
                A("TE", htro, wait=(f"z{tix}", "modpost", pfree["ps1"]), tag=f"htro{tix}")
                pfree["ps1"] = f"hevo{tix}"

                def heve(sc, tix=tix, i4=i4):
                    for fc in (0, 2, 4, 6):
                        inst = sc.activation(
                            hbf()[:, fc * 512 + i4 * 128:fc * 512 + i4 * 128 + 128],
                            ps0b[:, (fc // 2) * 128:(fc // 2) * 128 + 128], ACT.Identity,
                            bias=sa_v[:, fc:fc + 1], scale=onca[:, fc:fc + 1])
                    return inst
                A("SC", heve, wait=(f"htre{tix}",), tag=f"heve{tix}")

                def hevo(sc, tix=tix, i4=i4):
                    for fc in (1, 3, 5, 7):
                        inst = sc.activation(
                            hbf()[:, fc * 512 + i4 * 128:fc * 512 + i4 * 128 + 128],
                            ps1b[:, (fc // 2) * 128:(fc // 2) * 128 + 128], ACT.Identity,
                            bias=sa_v[:, fc:fc + 1], scale=onca[:, fc:fc + 1])
                    return inst
                A("SC", hevo, wait=(f"htro{tix}",), tag=f"hevo{tix}")

            hdone = tuple(f"hev{p}{g*4+i}" for p in ("e", "o") for i in range(4))

            for oc in range(8):
                if PA < 3:
                    break
                b = oc % 2
                prev = (f"qkev{g}_{oc-2}" if oc >= 2
                        else (f"qkev{g-1}_{6+oc}" if g > 0 else None))

                @dma(2)
                def wqk_ld(sy, oc=oc, b=b):
                    return [
                        sy.dma_start(out=wtq[b].rearrange("p (fc o) -> p fc o", fc=8),
                                     in_=wbf_h[0][:, oc * 128:(oc + 1) * 128]
                                     .rearrange("(fc p) o -> p fc o", p=128)),
                        sy.dma_start(out=wtk[b].rearrange("p (fc o) -> p fc o", fc=8),
                                     in_=wbf_h[1][:, oc * 128:(oc + 1) * 128]
                                     .rearrange("(fc p) o -> p fc o", p=128))]
                A("SY", wqk_ld, wait=("wstage", "modmm5") + ((prev,) if prev else ()),
                  tag=f"wqk{g}_{oc}", kind="d", dsem=f"QKW{b}")

                def qkmm(t, oc=oc, b=b):
                    for fc in range(8):
                        t.matmul(ps0[:], wtq[b][:, fc * 128:(fc + 1) * 128],
                                 hbf()[:, fc * 512:(fc + 1) * 512],
                                 start=(fc == 0), stop=(fc == 7))
                    for fc in range(8):
                        inst = t.matmul(ps1[:], wtk[b][:, fc * 128:(fc + 1) * 128],
                                        hbf()[:, fc * 512:(fc + 1) * 512],
                                        start=(fc == 0), stop=(fc == 7))
                    return inst
                A("TE", qkmm, wait=hdone + (f"wqk{g}_{oc}", pfree["ps0"], pfree["ps1"]),
                  tag=f"qkmm{g}_{oc}")
                pfree["ps0"] = f"qkev{g}_{oc}"
                pfree["ps1"] = f"qkev{g}_{oc}"

                def qkev(v, oc=oc, g=g):
                    for cl in range(4):
                        v.tensor_tensor(qsb()[:, oc * 512 + cl * 128:oc * 512 + cl * 128 + 128],
                                        ps0[:, cl * 128:(cl + 1) * 128],
                                        qscale[:, oc * 128:(oc + 1) * 128], ALU.mult)
                    for cl in range(4):
                        inst = v.tensor_tensor(ksb()[:, oc * 512 + cl * 128:oc * 512 + cl * 128 + 128],
                                               ps1[:, cl * 128:(cl + 1) * 128],
                                               kscale[:, oc * 128:(oc + 1) * 128], ALU.mult)
                    return inst
                A("VE", qkev, wait=(f"qkmm{g}_{oc}",), tag=f"qkev{g}_{oc}")

            for half in range(2):
                if PA < 4:
                    break
                if half == 0:
                    prevw = f"vtmm{g-1}_3_1" if g > 0 else None
                else:
                    prevw = f"vtmm{g}_3_0"

                @dma(1)
                def wv_ld(sy, half=half):
                    return sy.dma_start(out=wvw().rearrange("p (fc o) -> p fc o", fc=8),
                                        in_=wbf_h[2][:, half * 512:(half + 1) * 512]
                                        .rearrange("(fc p) o -> p fc o", p=128))
                A("SY", wv_ld, wait=("wstage",) + ((prevw,) if prevw else ()),
                  tag=f"wv{g}_{half}", kind="d", dsem="WVL")

                for cc in range(4):
                    ci = g * 4 + cc
                    pv, pvn = (ps2, "ps2") if cc % 2 == 0 else (ps3, "ps3")
                    bre = ((f"smm{(g-1)*4+cc}_7", f"ymm{(g-1)*4+cc}_7") if g > 0 else ())

                    def vtmm(t, cc=cc, half=half, pv=pv):
                        for fc in range(8):
                            inst = t.matmul(pv[:],
                                            hbf()[:, fc * 512 + cc * 128:fc * 512 + cc * 128 + 128],
                                            wvw()[:, fc * 512:(fc + 1) * 512],
                                            start=(fc == 0), stop=(fc == 7))
                        return inst
                    A("TE", vtmm, wait=hdone + (f"wv{g}_{half}", pfree[pvn]),
                      tag=f"vtmm{g}_{cc}_{half}")
                    pfree[pvn] = f"vtev{g}_{cc}_{half}"

                    def vtev(sc, cc=cc, half=half, pv=pv):
                        return sc.activation(vtb[cc][:, half * 512:(half + 1) * 512],
                                             pv[:], ACT.Copy)
                    A("SC", vtev, wait=(f"vtmm{g}_{cc}_{half}",) + bre,
                      tag=f"vtev{g}_{cc}_{half}")

            for cc in range(4):
                if PA < 5:
                    break
                ci = g * 4 + cc
                bb = ci % 2
                scur = S_all[ci % 2]
                snew = S_all[(ci + 1) % 2]
                sprev = f"sev{ci-1}" if ci > 0 else "init0"
                qkall = (f"qkev{g}_7",)

                kbre = (f"smm{ci-2}_7",) if (ci >= 2 and PA >= 8) else ()

                def ktr_e(t, cc=cc):
                    for oc in (0, 2, 4, 6):
                        inst = t.transpose(ps2b[:, (oc // 2) * 128:(oc // 2) * 128 + 128],
                                           ksb()[:, oc * 512 + cc * 128:oc * 512 + cc * 128 + 128],
                                           identbf[:])
                    return inst
                A("TE", ktr_e, wait=qkall + (pfree["ps2"],), tag=f"ktre{ci}")
                pfree["ps2"] = f"kteve{ci}"

                def ktr_o(t, cc=cc):
                    for oc in (1, 3, 5, 7):
                        inst = t.transpose(ps3b[:, (oc // 2) * 128:(oc // 2) * 128 + 128],
                                           ksb()[:, oc * 512 + cc * 128:oc * 512 + cc * 128 + 128],
                                           identbf[:])
                    return inst
                A("TE", ktr_o, wait=qkall + (pfree["ps3"],), tag=f"ktro{ci}")
                pfree["ps3"] = f"ktevo{ci}"

                KTEV = int(os.environ.get("KKTEV", "0"))
                if KTEV == 9:
                    S.alias(f"kteve{ci}", f"ktre{ci}")
                    S.alias(f"ktevo{ci}", f"ktro{ci}")

                def ktev_e(sc, bb=bb, KTEV=KTEV):
                    if KTEV == 1:
                        for q, oc in enumerate((0, 2, 4, 6)):
                            inst = sc.activation(ktb[bb][:, oc * 128:(oc + 1) * 128],
                                                 ps2b[:, q * 128:(q + 1) * 128], ACT.Copy)
                        return inst
                    return sc.activation(
                        ktb[bb].rearrange("p (o n) -> p o n", n=128)[:, 0:8:2, :],
                        ps2b[:, 0:512].rearrange("p (q n) -> p q n", q=4), ACT.Copy)
                if KTEV != 9:
                    A("SC", ktev_e, wait=(f"ktre{ci}",) + kbre, tag=f"kteve{ci}")

                def ktev_o(sc, bb=bb, KTEV=KTEV):
                    if KTEV == 1:
                        for q, oc in enumerate((1, 3, 5, 7)):
                            inst = sc.activation(ktb[bb][:, oc * 128:(oc + 1) * 128],
                                                 ps3b[:, q * 128:(q + 1) * 128], ACT.Copy)
                        return inst
                    return sc.activation(
                        ktb[bb].rearrange("p (o n) -> p o n", n=128)[:, 1:8:2, :],
                        ps3b[:, 0:512].rearrange("p (q n) -> p q n", q=4), ACT.Copy)
                if KTEV != 9:
                    A("SC", ktev_o, wait=(f"ktro{ci}",) + kbre, tag=f"ktevo{ci}")

                kvdone = (f"kteve{ci}", f"ktevo{ci}",
                          f"vtev{g}_{cc}_0", f"vtev{g}_{cc}_1")
                if PA < 6:
                    continue

                for hp in range(8):
                    pe_, pen = (ps0, "ps0") if hp % 2 == 0 else (ps2, "ps2")
                    po_, pon = (ps1, "ps1") if hp % 2 == 0 else (ps3, "ps3")

                    def amm(t, cc=cc, hp=hp, pe_=pe_, po_=po_):
                        sl = slice(hp * 512 + cc * 128, hp * 512 + cc * 128 + 128)
                        t.matmul(pe_[:, 0:128], ksb()[0:64, sl], qsb()[0:64, sl],
                                 start=True, stop=True)
                        return t.matmul(po_[:, 0:128], ksb()[64:128, sl], qsb()[64:128, sl],
                                        start=True, stop=True)
                    A("TE", amm, wait=qkall + (pfree[pen], pfree[pon]), tag=f"amm{ci}_{hp}")
                    pfree[pen] = f"amev{ci}_{hp}"
                    pfree[pon] = f"amev{ci}_{hp}"

                    if PA < 7:
                        ambre = ()
                    elif hp >= 2:
                        ambre = (f"ymm{ci}_{hp-2}",)
                    elif ci > 0:
                        ambre = (f"ymm{ci-1}_{hp+6}",)
                    else:
                        ambre = ()

                    def amev(v, ci=ci, hp=hp, pe_=pe_, po_=po_):
                        v.tensor_tensor(amb[hp % 2][:, 0:128], pe_[:, 0:128],
                                        causal2[:, 0:128], ALU.mult)
                        return v.tensor_tensor(amb[hp % 2][:, 128:256], po_[:, 0:128],
                                               causal2[:, 128:256], ALU.mult)
                    A("VE", amev, wait=(f"amm{ci}_{hp}",) + ambre, tag=f"amev{ci}_{hp}")

                    py, pyn = (ps4, "ps4") if hp < 4 else (ps5, "ps5")
                    col = (hp % 4) * 128

                    def ymm(t, ci=ci, cc=cc, hp=hp, py=py, col=col, scur=scur):
                        sl = slice(hp * 512 + cc * 128, hp * 512 + cc * 128 + 128)
                        t.matmul(py[0:64, col:col + 128],
                                 vtb[cc][:, hp * 128:hp * 128 + 64],
                                 amb[hp % 2][:, 0:128], start=True, stop=False)
                        t.matmul(py[0:64, col:col + 128], scur[:, 64 * hp:64 * hp + 64],
                                 qsb()[:, sl], start=False, stop=True)
                        t.matmul(py[64:128, col:col + 128],
                                 vtb[cc][:, hp * 128 + 64:hp * 128 + 128],
                                 amb[hp % 2][:, 128:256], start=True, stop=False,
                                 tile_position=(0, 64))
                        return t.matmul(py[64:128, col:col + 128],
                                        scur[:, 512 + 64 * hp:512 + 64 * hp + 64],
                                        qsb()[:, sl], start=False, stop=True,
                                        tile_position=(0, 64))
                    if PA < 7:
                        continue
                    sprev_y = sprev if PA >= 8 else "init0"
                    A("TE", ymm, wait=(f"amev{ci}_{hp}", sprev_y, pfree[pyn]) + kvdone,
                      tag=f"ymm{ci}_{hp}")
                    if hp % 4 == 3:
                        pfree[pyn] = f"yev{ci}_{hp//4}"

                    def smm(t, ci=ci, cc=cc, hp=hp, bb=bb, scur=scur):
                        t.matmul(psS[0:64, 64 * hp:64 * hp + 64],
                                 ktb[bb][:, hp * 128:hp * 128 + 64],
                                 vtb[cc][:, hp * 128:hp * 128 + 64], start=True, stop=False)
                        t.matmul(psS[0:64, 64 * hp:64 * hp + 64],
                                 identbf[:, 0:64], scur[:, 64 * hp:64 * hp + 64],
                                 start=False, stop=True)
                        t.matmul(psS[64:128, 512 + 64 * hp:512 + 64 * hp + 64],
                                 ktb[bb][:, hp * 128 + 64:hp * 128 + 128],
                                 vtb[cc][:, hp * 128 + 64:hp * 128 + 128],
                                 start=True, stop=False, tile_position=(0, 64))
                        return t.matmul(psS[64:128, 512 + 64 * hp:512 + 64 * hp + 64],
                                        identbf[:, 64:128],
                                        scur[:, 512 + 64 * hp:512 + 64 * hp + 64],
                                        start=False, stop=True, tile_position=(0, 64))
                    if PA < 8:
                        continue
                    A("TE", smm, wait=kvdone + (sprev, pfree["psS"]), tag=f"smm{ci}_{hp}")

                for bank in range(2 if PA >= 7 else 0):
                    py = ps4 if bank == 0 else ps5

                    def yevb(sc, ci=ci, bank=bank, py=py):
                        return sc.activation(
                            ybig.rearrange("p (hp t) -> p hp t", hp=8)[
                                :, bank * 4:bank * 4 + 4, ci * 128:ci * 128 + 128],
                            py[:, 0:512].rearrange("p (q n) -> p q n", q=4),
                            ACT.Copy)
                    A("SC", yevb, wait=(f"ymm{ci}_{bank*4+3}",), tag=f"yev{ci}_{bank}")

                def sev(v, ci=ci, snew=snew):
                    v.tensor_tensor(snew[0:64, 0:512], psS[0:64, 0:512],
                                    gcscale[0:64, 0:512], ALU.mult)
                    return v.tensor_tensor(snew[64:128, 512:1024], psS[64:128, 512:1024],
                                           gcscale[64:128, 512:1024], ALU.mult)
                if PA >= 8:
                    A("VE", sev, wait=tuple(f"smm{ci}_{hp}" for hp in range(8)), tag=f"sev{ci}")
                    pfree["psS"] = f"sev{ci}"

                run_quant_chunk()

        if DBG < 3:
            if DUMP:
                emit_dumps()
            snap2 = dict(S.cnt)

            def _dfin(gp, snap2=snap2):
                for kk in tuple(S.dsems):
                    if snap2.get(kk, 0):
                        gp.wait_ge(S.sems[kk], snap2.get(kk, 0))
                return gp.engine_nop()
            A("GP", _dfin)
            S.emit(E(nc.Block()))
            return nc



        if DBG < 4:
            if DUMP:
                emit_dumps()
            snap2 = dict(S.cnt)

            def _dfin(gp, snap2=snap2):
                for kk in tuple(S.dsems):
                    if snap2.get(kk, 0):
                        gp.wait_ge(S.sems[kk], snap2.get(kk, 0))
                return gp.engine_nop()
            A("GP", _dfin)
            S.emit(E(nc.Block()))
            return nc

        for _ in iter(lambda: next(quant_gen, StopIteration), StopIteration):
            pass

        retdone = (f"yev{NCH-1}_1", f"sev{NCH-1}")
        for hp in range(8):
            def gnstats(v, hp=hp):
                nb = T // 512
                for k in range(nb):
                    S.bar(v.bn_stats(gnst[:, k * 6:(k + 1) * 6],
                          ybig[:, hp * T + k * 512:hp * T + k * 512 + 512]))
                ncols = nb * 6
                me = gnst[:, 1:ncols:6]; mo = gnst[:, 4:ncols:6]
                m2e = gnst[:, 2:ncols:6]; m2o = gnst[:, 5:ncols:6]
                S.bar(v.tensor_tensor(stats[:, 48:48 + nb], me, mo, ALU.add))
                S.bar(v.tensor_reduce(col1[:], stats[:, 48:48 + nb], AX, ALU.add))
                S.bar(v.tensor_scalar(gnsy[:, hp:hp + 1], col1[:], 256.0, None, ALU.mult))
                S.bar(v.tensor_tensor(stats[:, 48:48 + nb], m2e, m2o, ALU.add))
                S.bar(v.tensor_reduce(col1[:], stats[:, 48:48 + nb], AX, ALU.add))
                S.bar(v.tensor_tensor(stats[:, 48:48 + nb], me, me, ALU.mult))
                S.bar(v.tensor_tensor(stats[:, 56:56 + nb], mo, mo, ALU.mult))
                S.bar(v.tensor_tensor(stats[:, 48:48 + nb], stats[:, 48:48 + nb],
                      stats[:, 56:56 + nb], ALU.add))
                S.bar(v.tensor_reduce(tmp8[:, 0:1], stats[:, 48:48 + nb], AX, ALU.add))
                return v.scalar_tensor_tensor(gnsyy[:, hp:hp + 1], tmp8[:, 0:1], 256.0,
                                              col1[:], ALU.mult, ALU.add)
            A("VE", gnstats, wait=retdone + ("dqdone", "dqdoneB"), tag=f"gnst{hp}")

        def gntr(t):
            t.transpose(ps0[0:8, 0:128], gnsy[:], ident32[:])
            return t.transpose(ps0[0:8, 128:256], gnsyy[:], ident32[:])
        A("TE", gntr, wait=tuple(f"gnst{hp}" for hp in range(8)) + (pfree["ps0"],),
          tag="gntr")
        pfree["ps0"] = "gnhead"

        def gnhead(v):
            S.bar(v.tensor_reduce(hse[:, 0:1], ps0[0:8, 0:64], AX, ALU.add))
            S.bar(v.tensor_reduce(hso[:, 0:1], ps0[0:8, 64:128], AX, ALU.add))
            S.bar(v.tensor_reduce(hse[:, 1:2], ps0[0:8, 128:192], AX, ALU.add))
            S.bar(v.tensor_reduce(hso[:, 1:2], ps0[0:8, 192:256], AX, ALU.add))
            NT = float(64 * T)
            for hs in (hse, hso):
                S.bar(v.tensor_scalar(hs[:, 2:3], hs[:, 0:1], 1.0 / NT, None, ALU.mult))
                S.bar(v.tensor_scalar(hs[:, 3:4], hs[:, 1:2], 1.0 / NT, None, ALU.mult))
                S.bar(v.tensor_tensor(hs[:, 4:5], hs[:, 2:3], hs[:, 2:3], ALU.mult))
                inst = v.tensor_tensor(hs[:, 3:4], hs[:, 3:4], hs[:, 4:5], ALU.subtract)
            return inst
        A("VE", gnhead, wait=("gntr",), tag="gnhead")

        def gnsqrt(sc):
            sc.activation(hse[:, 3:4], hse[:, 3:4], ACT.Sqrt, bias=epsv[0:8, :])
            return sc.activation(hso[:, 3:4], hso[:, 3:4], ACT.Sqrt, bias=epsv[0:8, :])
        A("SC", gnsqrt, wait=("gnhead",), tag="gnsqrt")

        def gnrstd(v):
            for hs in (hse, hso):
                S.bar(v.reciprocal(hs[:, 5:6], hs[:, 3:4]))
                inst = v.tensor_tensor(hs[:, 6:7], hs[:, 2:3], hs[:, 5:6], ALU.mult)
            return inst
        A("VE", gnrstd, wait=("gnsqrt",), tag="gnrstd")

        def gnexp(t):
            for oc in range(8):
                pst = ps0 if oc % 2 == 0 else ps1
                c0 = 2 * (oc // 2)
                t.matmul(pst[:, c0:c0 + 2], sel_e[:, oc * 128:(oc + 1) * 128],
                         hse[:, 5:7], start=True, stop=False)
                inst = t.matmul(pst[:, c0:c0 + 2], sel_o[:, oc * 128:(oc + 1) * 128],
                                hso[:, 5:7], start=False, stop=True)
            return inst
        A("TE", gnexp, wait=("gnrstd", pfree["ps0"], pfree["ps1"]), tag="gnexp")
        pfree["ps0"] = "gnab"
        pfree["ps1"] = "gnab"

        def gnab(v):
            for oc in range(8):
                pst = ps0 if oc % 2 == 0 else ps1
                c0 = 2 * (oc // 2)
                S.bar(v.tensor_copy(tmp8[:, oc:oc + 1], pst[:, c0:c0 + 1]))
                S.bar(v.tensor_copy(tmp8[:, 8 + oc:9 + oc], pst[:, c0 + 1:c0 + 2]))
            S.bar(v.tensor_tensor(gnA[:], tmp8[:, 0:8], gnwv[:], ALU.mult))
            S.bar(v.tensor_tensor(gnB[:], tmp8[:, 8:16], gnwv[:], ALU.mult))
            return v.tensor_tensor(gnB[:], gnbv[:], gnB[:], ALU.subtract)
        A("VE", gnab, wait=("gnexp",), tag="gnab")

        @dma(2)
        def wo_ld(sy):
            return [sy.dma_start(out=wkw().rearrange("p (fc o) -> p fc o", fc=8),
                                 in_=wbf_h[3][:, 0:512].rearrange("(fc p) o -> p fc o", p=128)),
                    sy.dma_start(out=wvw().rearrange("p (fc o) -> p fc o", fc=8),
                                 in_=wbf_h[3][:, 512:1024].rearrange("(fc p) o -> p fc o", p=128))]
        A("SY", wo_ld, wait=(f"vtmm{NG-1}_3_1", "wstage", f"qkmm{NG-1}_7", f"ymm{NCH-1}_7", f"smm{NCH-1}_7"), tag="wo_ld", kind="d", dsem="WOL")

        for ci in range(NCH):
            b = ci % 2

            def ynf(v, ci=ci):
                for hp in range(8):
                    inst = v.tensor_scalar(ynbs[ci % 2][:, hp * 128:(hp + 1) * 128],
                                           ybig[:, hp * T + ci * 128:hp * T + ci * 128 + 128],
                                           gnA[:, hp:hp + 1], gnB[:, hp:hp + 1],
                                           ALU.mult, ALU.add)
                return inst
            A("VE", ynf, wait=("gnab",) + ((f"womm{ci-2}",) if ci > 1 else ()),
              tag=f"yn{ci}")

            @dma(1)
            def x2ld(sy, ci=ci, b4=ci % 4):
                return sy.dma_start(out=xbuf[b4], in_=x_h[ci * 128:(ci + 1) * 128, :])
            A("SY", x2ld, wait=((f"resid{ci-4}",) if ci >= 4 else (f"z{NCH-1}",)),
              tag=f"x2ld{ci}", kind="d", dsem=f"XB{ci % 4}")

            pw0, pw0n = (ps0, "ps0") if ci % 2 == 0 else (ps4, "ps4")
            pw1, pw1n = (ps1, "ps1") if ci % 2 == 0 else (ps5, "ps5")

            def womm(t, ci=ci, pw0=pw0, pw1=pw1):
                for fc in range(8):
                    t.matmul(pw0[:], ynbs[ci % 2][:, fc * 128:(fc + 1) * 128],
                             wkw()[:, fc * 512:(fc + 1) * 512],
                             start=(fc == 0), stop=(fc == 7))
                for fc in range(8):
                    inst = t.matmul(pw1[:], ynbs[ci % 2][:, fc * 128:(fc + 1) * 128],
                                    wvw()[:, fc * 512:(fc + 1) * 512],
                                    start=(fc == 0), stop=(fc == 7))
                return inst
            A("TE", womm, wait=(f"yn{ci}", "wo_ld", pfree[pw0n], pfree[pw1n]),
              tag=f"womm{ci}")
            pfree[pw0n] = f"residv{ci}"
            pfree[pw1n] = f"residv{ci}"

            def residv(v, ci=ci, b=b, pw0=pw0, pw1=pw1):
                v.tensor_tensor(x1b[b]()[:, 0:512], pw0[:], ga_bc[:, 0:512], ALU.mult)
                return v.tensor_tensor(x1b[b]()[:, 512:1024], pw1[:], ga_bc[:, 512:1024], ALU.mult)
            A("VE", residv, wait=(f"womm{ci}", "gabc")
              + ((f"x1st{ci-2}", f"l2z{ci-2}") if ci >= 2 else ()), tag=f"residv{ci}")

            def resid(gp, ci=ci, b=b):
                return gp.tensor_tensor(x1b[b]()[:, 0:1024], x1b[b]()[:, 0:1024],
                                        xbuf[ci % 4], ALU.add)
            A("GP", resid, wait=(f"residv{ci}", f"x2ld{ci}"), tag=f"resid{ci}")

            @dma(1)
            def x1st(sc, ci=ci, b=b):
                return sc.dma_start(out=x1sp_h[ci], in_=x1b[b]()[:, 0:1024])
            A("SC", x1st, wait=(f"resid{ci}",), tag=f"x1st{ci}", kind="d", dsem=f"XS{ci % 2}")

            def l2stats(v, ci=ci, b=b):
                mean = stat2[:, 0:1]; var = stat2[:, 4:5]
                S.bar(v.tensor_tensor(l2tmp[:], x1b[b]()[:, 0:1024],
                                      x1b[b]()[:, 0:1024], ALU.mult))
                S.bar(v.tensor_reduce(mean, x1b[b]()[:, 0:1024], AX, ALU.add))
                S.bar(v.tensor_reduce(var, l2tmp[:], AX, ALU.add))
                S.bar(v.tensor_scalar(mean, mean, 1.0 / 1024.0, None, ALU.mult))
                S.bar(v.tensor_scalar(var, var, 1.0 / 1024.0, None, ALU.mult))
                S.bar(v.tensor_tensor(tmp8[:, 0:1], mean, mean, ALU.mult))
                return v.tensor_tensor(var, var, tmp8[:, 0:1], ALU.subtract)
            A("VE", l2stats, wait=(f"resid{ci}",), tag=f"l2s{ci}")

            def l2sqrt(sc, ci=ci):
                return sc.activation(stat2[:, 4:5], stat2[:, 4:5], ACT.Sqrt, bias=epsv[:])
            A("SC", l2sqrt, wait=(f"l2s{ci}",), tag=f"l2q{ci}")

            def l2r(v, ci=ci):
                S.bar(v.reciprocal(stat2[:, 8:9], stat2[:, 4:5]))
                S.bar(v.tensor_tensor(stat2[:, 12:13], stat2[:, 0:1], stat2[:, 8:9], ALU.mult))
                return v.tensor_scalar(stat2[:, 12:13], stat2[:, 12:13], -1.0, None, ALU.mult)
            A("VE", l2r, wait=(f"l2q{ci}",), tag=f"l2r{ci}")

            def l2z(sc, ci=ci, b=b):
                return sc.activation(zbuf[b][:], x1b[b]()[:, 0:1024], ACT.Identity,
                                     scale=stat2[:, 8:9], bias=stat2[:, 12:13])
            A("SC", l2z, wait=(f"l2r{ci}",) + ((f"h2tro{ci-2}",) if ci >= 2 else ()),
              tag=f"l2z{ci}")

            def h2tre(t, ci=ci, b=b):
                for fc in (0, 2, 4, 6):
                    inst = t.transpose(ps2b[:, (fc // 2) * 128:(fc // 2) * 128 + 128],
                                       zbuf[b][:, fc * 128:(fc + 1) * 128], identbf[:])
                return inst
            A("TE", h2tre, wait=(f"l2z{ci}", pfree["ps2"]), tag=f"h2tre{ci}")
            pfree["ps2"] = f"h2eve{ci}"

            def h2tro(t, ci=ci, b=b):
                for fc in (1, 3, 5, 7):
                    inst = t.transpose(ps3b[:, (fc // 2) * 128:(fc // 2) * 128 + 128],
                                       zbuf[b][:, fc * 128:(fc + 1) * 128], identbf[:])
                return inst
            A("TE", h2tro, wait=(f"l2z{ci}", pfree["ps3"]), tag=f"h2tro{ci}")
            pfree["ps3"] = f"h2evo{ci}"

            def h2eve(sc, ci=ci, b=b):
                for fc in (0, 2, 4, 6):
                    inst = sc.activation(h2b[b][:, fc * 128:(fc + 1) * 128],
                                         ps2b[:, (fc // 2) * 128:(fc // 2) * 128 + 128],
                                         ACT.Identity, bias=sbv[:, fc:fc + 1],
                                         scale=oncb[:, fc:fc + 1])
                return inst
            A("SC", h2eve, wait=(f"h2tre{ci}",) + ((f"h2rdfree{ci-2}",) if ci >= 2 else ()),
              tag=f"h2eve{ci}")

            def h2evo(sc, ci=ci, b=b):
                for fc in (1, 3, 5, 7):
                    inst = sc.activation(h2b[b][:, fc * 128:(fc + 1) * 128],
                                         ps3b[:, (fc // 2) * 128:(fc // 2) * 128 + 128],
                                         ACT.Identity, bias=sbv[:, fc:fc + 1],
                                         scale=oncb[:, fc:fc + 1])
                return inst
            A("SC", h2evo, wait=(f"h2tro{ci}",), tag=f"h2evo{ci}")
            S.alias(f"h2rdfree{ci}", f"h2st{ci}") if False else None

            def h2max(v, ci=ci, b=b):
                return v.tensor_reduce(mxb[:, ci:ci + 1], h2b[b], AX, ALU.max,
                                       apply_absolute_value=True)
            A("VE", h2max, wait=(f"h2eve{ci}", f"h2evo{ci}"), tag=f"h2max{ci}")

            @dma(1)
            def h2st(sc, ci=ci, b=b):
                return sc.dma_start(out=h2sp_h[ci], in_=h2b[b])
            A("SC", h2st, wait=(f"h2eve{ci}", f"h2evo{ci}"), tag=f"h2st{ci}", kind="d",
              dsem=f"HS{ci % 2}")
            S.alias(f"h2rdfree{ci}", f"h2st{ci}")

        if DBG < 5:
            if DUMP:
                emit_dumps()
            snap2 = dict(S.cnt)

            def _dfin(gp, snap2=snap2):
                for kk in tuple(S.dsems):
                    if snap2.get(kk, 0):
                        gp.wait_ge(S.sems[kk], snap2.get(kk, 0))
                return gp.engine_nop()
            A("GP", _dfin)
            S.emit(E(nc.Block()))
            return nc

        def mx1fold(v):
            return v.tensor_reduce(col1[:], mxb[:], AX, ALU.max)
        A("VE", mx1fold, wait=(f"h2max{NCH-1}",), tag="mx1fold")

        def mx1tr(t):
            return t.transpose(ps0[0:1, 0:128], col1[:], ident32[:])
        A("TE", mx1tr, wait=("mx1fold", pfree["ps0"]), tag="mx1tr")
        pfree["ps0"] = "mx1"

        def mx1(v):
            return v.tensor_reduce(scal[0:1, 0:1], ps0[0:1, 0:128], AX, ALU.max)
        A("VE", mx1, wait=("mx1tr",), tag="mx1")

        @dma(1)
        def cc1w(gp):
            return gp.dma_start(out=ccin_h, in_=scal[:])
        A("GP", cc1w, wait=("mx1",), tag="cc1w", kind="d")

        def cc1(gp):
            return gp.collective_compute("AllReduce", ALU.max,
                                         replica_groups=[list(range(NCORE))],
                                         ins=[ccin_h.opt()], outs=[ccout_h.opt()])
        A("GP", cc1, wait=("cc1w",), tag="cc1")

        @dma(1)
        def cc1r(gp):
            return gp.dma_start(out=scal2[:], in_=ccout_h)
        A("GP", cc1r, wait=("cc1",), tag="cc1r", kind="d")

        def sigx(sc):
            return sc.activation(scal[0:1, 1:2], clipx[:], ACT.Sigmoid)
        A("SC", sigx, wait=("ldc", "cc1w"), tag="sigx")

        def rxchain(v):
            S.bar(v.tensor_tensor(scal[0:1, 2:3], scal[0:1, 1:2], scal2[0:1, 0:1], ALU.mult))
            S.bar(v.reciprocal(scal[0:1, 3:4], scal[0:1, 2:3]))
            S.bar(v.tensor_scalar(scal[0:1, 4:5], scal[0:1, 3:4], QMAX, None, ALU.mult))
            return v.tensor_scalar(scal[0:1, 5:6], scal[0:1, 2:3], 1.0 / 16129.0,
                                   None, ALU.mult)
        A("VE", rxchain, wait=("cc1r", "sigx"), tag="rxchain")

        def rxbc(t):
            t.matmul(ps0[:, 0:1], ones1[:], scal[0:1, 4:5], start=True, stop=True)
            return t.matmul(ps0[:, 1:2], ones1[:], scal[0:1, 5:6], start=True, stop=True)
        A("TE", rxbc, wait=("rxchain", pfree["ps0"]), tag="rxbc")
        pfree["ps0"] = "rxv"

        def rxv(v):
            S.bar(v.tensor_copy(rx_vec[:], ps0[:, 0:1]))
            S.bar(v.tensor_scalar(lamg[:], mg_fm[:, 0:NIC], ps0[:, 1:2], None, ALU.mult))
            return v.tensor_scalar(lamu[:], mu_fm[:, 0:NIC], ps0[:, 1:2], None, ALU.mult)
        A("VE", rxv, wait=("rxbc", "gqdone", "gqdoneB", "uqdone", "uqdoneB"), tag="rxv")

        if DBG < 6:
            if DUMP:
                emit_dumps()
            snap2 = dict(S.cnt)

            def _dfin(gp, snap2=snap2):
                for kk in tuple(S.dsems):
                    if snap2.get(kk, 0):
                        gp.wait_ge(S.sems[kk], snap2.get(kk, 0))
                return gp.engine_nop()
            A("GP", _dfin)
            S.emit(E(nc.Block()))
            return nc

        for ci in range(NCH):
            b = ci % 2

            @dma(1)
            def h2rd(sy, ci=ci, b=b):
                return sy.dma_start(out=h2b[b], in_=h2sp_h[ci])
            A("SY", h2rd, wait=(f"h2st{NCH-1}", f"h2st{NCH-2}", "mx1")
              + ((f"ixq{ci-2}",) if ci >= 2 else ()),
              tag=f"h2rd{ci}", kind="d", dsem=f"HRD{ci % 2}")

            def ixq12(v, ci=ci, b=b):
                S.bar(v.tensor_scalar(h2b[b], h2b[b], rx_vec[:], MAGIC, ALU.mult, ALU.add))
                return v.tensor_scalar(h2b[b], h2b[b], MAGIC - 128.0, MAGIC + QMAX,
                                       ALU.max, ALU.min)
            A("VE", ixq12, wait=(f"h2rd{ci}", "rxv"), tag=f"ixq12_{ci}")

            def ixq3(sc, ci=ci, b=b):
                return sc.activation(
                    ybig.rearrange("p (fc t) -> p fc t", fc=8)[:, :, ci * 128:(ci + 1) * 128],
                    h2b[b].rearrange("p (fc j) -> p fc j", fc=8),
                    ACT.Identity, bias=negmg[:])
            A("SC", ixq3, wait=(f"ixq12_{ci}",), tag=f"ixq{ci}")

        ixdone = (f"ixq{NCH-1}",)

        for ic in range(NIC):
            b = ic % 2

            @dma(2)
            def igld(sy, ic=ic, b=b):
                return [sy.dma_start(out=igb[b]()[:, 0:1024].rearrange("p (fc n) -> p fc n", fc=8),
                                     in_=igst_h[ic].rearrange("fc p n -> p fc n")),
                        sy.dma_start(out=igb[b]()[:, 1024:2048].rearrange("p (fc n) -> p fc n", fc=8),
                                     in_=iust_h[ic].rearrange("fc p n -> p fc n"))]
            A("SY", igld, wait=("gqdone", "gqdoneB", "uqdone", "uqdoneB", f"ymm{NCH-1}_7")
              + ((f"gumm{ic-2}_{NTT-1}",) if ic >= 2 else ()), tag=f"igld{ic}",
              kind="d", dsem=f"IGL{ic % 2}")

            for tt in range(NTT):
                pg, pgn = (ps0, "ps0") if tt % 2 == 0 else (ps1, "ps1")
                pu, pun = (ps2, "ps2") if tt % 2 == 0 else (ps3, "ps3")

                def gumm(t, ic=ic, b=b, tt=tt, pg=pg, pu=pu):
                    for fc in range(8):
                        t.matmul(pg[:], igb[b]()[:, fc * 128:(fc + 1) * 128],
                                 ybig[:, fc * T + tt * 512:fc * T + tt * 512 + 512],
                                 start=(fc == 0), stop=(fc == 7))
                    for fc in range(8):
                        inst = t.matmul(pu[:], igb[b]()[:, 1024 + fc * 128:1024 + (fc + 1) * 128],
                                        ybig[:, fc * T + tt * 512:fc * T + tt * 512 + 512],
                                        start=(fc == 0), stop=(fc == 7))
                    return inst
                A("TE", gumm, wait=ixdone + (f"igld{ic}", pfree[pgn], pfree[pun]),
                  tag=f"gumm{ic}_{tt}")
                pfree[pgn] = f"silu{ic}_{tt}"
                pfree[pun] = f"tmul{ic}_{tt}"

                def siluf(sc, ic=ic, tt=tt, pg=pg):
                    return sc.activation(sgl[tt % 2]()[:], pg[:], ACT.Silu,
                                         scale=lamg[:, ic:ic + 1])
                A("SC", siluf, wait=(f"gumm{ic}_{tt}", "rxv")
                  + ((f"tmul{ic-1}_{tt}",) if ic > 0 else ()), tag=f"silu{ic}_{tt}")

                def tmul(v, ic=ic, tt=tt, pu=pu):
                    return v.scalar_tensor_tensor(tst[tt % 2]()[:], pu[:],
                                                  lamu[:, ic:ic + 1], sgl[tt % 2]()[:],
                                                  ALU.mult, ALU.mult)
                A("VE", tmul, wait=(f"silu{ic}_{tt}",)
                  + ((f"tst{ic-1}_{tt}", f"tmax{ic-1}_{tt}") if ic > 0 else ()),
                  tag=f"tmul{ic}_{tt}")

                def tmaxf(v, ic=ic, tt=tt):
                    return v.tensor_reduce(mxd[:, ic * NTT + tt:ic * NTT + tt + 1],
                                           tst[tt % 2]()[:], AX, ALU.max,
                                           apply_absolute_value=True)
                A("VE", tmaxf, wait=(f"tmul{ic}_{tt}",), tag=f"tmax{ic}_{tt}")

                @dma(1)
                def tstw(sc, ic=ic, tt=tt):
                    return sc.dma_start(
                        out=tsp_h[4 * tt:4 * tt + 4, ic * 128:(ic + 1) * 128, :]
                        .rearrange("c i j -> i c j"),
                        in_=tst[tt % 2]().rearrange("i (c j) -> i c j", c=4))
                A("SC", tstw, wait=(f"tmul{ic}_{tt}",), tag=f"tst{ic}_{tt}", kind="d",
                  dsem=f"TSW{tt}")

        if DBG < 7:
            if DUMP:
                emit_dumps()
            snap2 = dict(S.cnt)

            def _dfin(gp, snap2=snap2):
                for kk in tuple(S.dsems):
                    if snap2.get(kk, 0):
                        gp.wait_ge(S.sems[kk], snap2.get(kk, 0))
                return gp.engine_nop()
            A("GP", _dfin)
            S.emit(E(nc.Block()))
            return nc

        def mx2fold(v):
            return v.tensor_reduce(col1[:], mxd[:, 0:NIC * NTT], AX, ALU.max)
        A("VE", mx2fold, wait=tuple(f"tmax{NIC-1}_{tt}" for tt in range(NTT))
          + tuple(f"tmax{ic}_{NTT-1}" for ic in range(NIC)), tag="mx2fold")

        def mx2tr(t):
            return t.transpose(ps4[0:1, 0:128], col1[:], ident32[:])
        A("TE", mx2tr, wait=("mx2fold", pfree["ps4"]), tag="mx2tr")
        pfree["ps4"] = "mx2"

        def mx2(v):
            return v.tensor_reduce(scal2[0:1, 8:9], ps4[0:1, 0:128], AX, ALU.max)
        A("VE", mx2, wait=("mx2tr",), tag="mx2")

        @dma(1)
        def cc2w(gp):
            return gp.dma_start(out=ccin_h, in_=scal2[:])
        A("GP", cc2w, wait=("mx2",), tag="cc2w", kind="d")

        def cc2(gp):
            return gp.collective_compute("AllReduce", ALU.max,
                                         replica_groups=[list(range(NCORE))],
                                         ins=[ccin_h.opt()], outs=[ccout_h.opt()])
        A("GP", cc2, wait=("cc2w",), tag="cc2")

        @dma(1)
        def cc2r(gp):
            return gp.dma_start(out=scal2[:], in_=ccout_h)
        A("GP", cc2r, wait=("cc2",), tag="cc2r", kind="d")

        def sigd(sc):
            return sc.activation(scal[0:1, 9:10], clipd[:], ACT.Sigmoid)
        A("SC", sigd, wait=("ldc", "cc2w"), tag="sigd")

        def rdchain(v):
            S.bar(v.tensor_tensor(scal[0:1, 10:11], scal[0:1, 9:10], scal2[0:1, 8:9], ALU.mult))
            S.bar(v.reciprocal(scal[0:1, 11:12], scal[0:1, 10:11]))
            S.bar(v.tensor_scalar(scal[0:1, 12:13], scal[0:1, 11:12], QMAX, None, ALU.mult))
            return v.tensor_scalar(scal[0:1, 13:14], scal[0:1, 10:11], 1.0 / 16129.0,
                                   None, ALU.mult)
        A("VE", rdchain, wait=("cc2r", "sigd"), tag="rdchain")

        def rdbc(t):
            t.matmul(ps4[:, 0:1], ones1[:], scal[0:1, 12:13], start=True, stop=True)
            return t.matmul(ps4[:, 1:2], ones1[:], scal[0:1, 13:14], start=True, stop=True)
        A("TE", rdbc, wait=("rdchain", pfree["ps4"]), tag="rdbc")
        pfree["ps4"] = "rdv"

        def rdv(v):
            S.bar(v.tensor_copy(rd_vec[:], ps4[:, 0:1]))
            S.bar(v.tensor_tensor(lamfm[:], gb_v[:], md_fm[:, 0:8], ALU.mult))
            return v.tensor_scalar(lamfm[:], lamfm[:], ps4[:, 1:2], None, ALU.mult)
        A("VE", rdv, wait=("rdbc", "dqdone", "dqdoneB"), tag="rdv")

        def lamtr(t):
            return t.transpose(ps4[0:8, 0:128], lamfm[:], ident32[:])
        A("TE", lamtr, wait=("rdv",), tag="lamtr")
        pfree["ps4"] = "lam8"

        def lam8(v):
            return v.tensor_copy(r22[0:8, :], ps4[0:8, 0:128])
        A("VE", lam8, wait=("lamtr", "drrow"), tag="lam8")

        @dma(1)
        def lamrow_w(gp):
            return gp.dma_start(out=rowb_h[0:1, 0:1024], in_=r22[0:8, :])
        A("GP", lamrow_w, wait=("lam8", "drrow"), tag="lamroww", kind="d", dsem="BNC")

        @dma(1)
        def lamrow_d(gp):
            return gp.dma_start(out=lamrow, in_=rowb_h[0:1, 0:1024])
        A("GP", lamrow_d, wait=("lamroww",), tag="lamrow", kind="d", dsem="BNC")

        def lambcm(t):
            t.matmul(ps4[:, 0:512], ones1[:], lamrow[0:1, 0:512], start=True, stop=True)
            return t.matmul(ps5[:, 0:512], ones1[:], lamrow[0:1, 512:1024], start=True, stop=True)
        A("TE", lambcm, wait=("lamrow", pfree["ps4"], pfree["ps5"]), tag="lambcm")
        pfree["ps4"] = "lambc"
        pfree["ps5"] = "lambc"

        def lambc(v):
            v.tensor_copy(lam_bc[:, 0:512], ps4[:])
            return v.tensor_copy(lam_bc[:, 512:1024], ps5[:])
        A("VE", lambc, wait=("lambcm",), tag="lambc")

        @dma(2)
        def idld(sy):
            return [sy.dma_start(out=idres()[:, 0:11 * 1024].rearrange("p (ic o) -> p ic o", ic=11),
                                 in_=idst_h[0:11].rearrange("ic p o -> p ic o")),
                    sy.dma_start(out=idres()[:, 11 * 1024:NIC * 1024].rearrange("p (ic o) -> p ic o", ic=11),
                                 in_=idst_h[11:NIC].rearrange("ic p o -> p ic o"))]
        A("SY", idld, wait=("dqdone", "dqdoneB", f"gumm{NIC-1}_{NTT-1}"), tag="idld", kind="d",
          dsem="IDL")

        if DBG < 8:
            if DUMP:
                emit_dumps()
            snap2 = dict(S.cnt)

            def _dfin(gp, snap2=snap2):
                for kk in tuple(S.dsems):
                    if snap2.get(kk, 0):
                        gp.wait_ge(S.sems[kk], snap2.get(kk, 0))
                return gp.engine_nop()
            A("GP", _dfin)
            S.emit(E(nc.Block()))
            return nc

        for ci in range(NCH):
            b = ci % 2

            for hf in range(2):
                @dma(1)
                def tccld(sy, ci=ci, hf=hf):
                    return sy.dma_start(
                        out=tcc[hf]().rearrange("p (ic j) -> p ic j", ic=11),
                        in_=tsp_h[ci, hf * 1408:(hf + 1) * 1408, :]
                        .rearrange("(ic p) j -> p ic j", p=128))
                A("SY" if hf == 0 else "SC", tccld, wait=(f"tst{NIC-1}_{ci//4}",)
                  + ((f"ihq{ci-1}_{hf}",) if ci >= 1 else ()),
                  tag=f"tcc{ci}_{hf}", kind="d", dsem=f"TCC{hf}")

                def ihq12(v, ci=ci, hf=hf):
                    S.bar(v.tensor_scalar(tcc[hf]()[:], tcc[hf]()[:], rd_vec[:], MAGIC,
                          ALU.mult, ALU.add))
                    return v.tensor_scalar(tcc[hf]()[:], tcc[hf]()[:], MAGIC - 128.0,
                                           MAGIC + QMAX, ALU.max, ALU.min)
                A("VE", ihq12, wait=(f"tcc{ci}_{hf}", "rdv"), tag=f"ihq12_{ci}_{hf}")

                def ihq3(sc, ci=ci, hf=hf, b=b):
                    return sc.activation(ihc[b][:, hf * 1408:(hf + 1) * 1408],
                                         tcc[hf]()[:], ACT.Identity, bias=negmg[:])
                A("SC", ihq3, wait=(f"ihq12_{ci}_{hf}",)
                  + ((f"dmm{ci-2}",) if ci >= 2 else ()), tag=f"ihq{ci}_{hf}")

            @dma(1)
            def x1ld(sy, ci=ci, b=b):
                return sy.dma_start(out=x1b[b]()[:, 0:1024], in_=x1sp_h[ci])
            A("SY", x1ld, wait=(f"x1st{NCH-1}", f"x1st{NCH-2}")
              + ((f"outev{ci-2}",) if ci >= 2 else ()), tag=f"x1ld{ci}", kind="d",
              dsem=f"X1L{ci % 2}")

            pd0, pd0n = (ps0, "ps0") if ci % 2 == 0 else (ps2, "ps2")
            pd1, pd1n = (ps1, "ps1") if ci % 2 == 0 else (ps3, "ps3")

            def dmm(t, ci=ci, b=b, pd0=pd0, pd1=pd1):
                for ic in range(NIC):
                    t.matmul(pd0[:], ihc[b][:, ic * 128:(ic + 1) * 128],
                             idres()[:, ic * 1024:ic * 1024 + 512],
                             start=(ic == 0), stop=(ic == NIC - 1))
                for ic in range(NIC):
                    inst = t.matmul(pd1[:], ihc[b][:, ic * 128:(ic + 1) * 128],
                                    idres()[:, ic * 1024 + 512:ic * 1024 + 1024],
                                    start=(ic == 0), stop=(ic == NIC - 1))
                return inst
            A("TE", dmm, wait=(f"ihq{ci}_0", f"ihq{ci}_1", "idld",
                               pfree[pd0n], pfree[pd1n]), tag=f"dmm{ci}")
            pfree[pd0n] = f"outevv{ci}"
            pfree[pd1n] = f"outevv{ci}"

            def outevv(v, ci=ci, b=b, pd0=pd0, pd1=pd1):
                v.tensor_tensor(outb[b]()[:, 0:512], pd0[:], lam_bc[:, 0:512], ALU.mult)
                return v.tensor_tensor(outb[b]()[:, 512:1024], pd1[:], lam_bc[:, 512:1024], ALU.mult)
            A("VE", outevv, wait=(f"dmm{ci}", "lambc")
              + ((f"outst{ci-2}",) if ci >= 2 else ()), tag=f"outevv{ci}")

            def outev(gp, ci=ci, b=b):
                return gp.tensor_tensor(outb[b]()[:, 0:1024], outb[b]()[:, 0:1024],
                                        x1b[b]()[:, 0:1024], ALU.add)
            A("GP", outev, wait=(f"outevv{ci}", f"x1ld{ci}"), tag=f"outev{ci}")

            @dma(1)
            def outst(sy, ci=ci, b=b):
                return sy.dma_start(out=out_h[ci * 128:(ci + 1) * 128, :],
                                    in_=outb[b]()[:, 0:1024])
            A("SY", outst, wait=(f"outev{ci}",), tag=f"outst{ci}", kind="d",
              dsem=f"OST{ci % 2}")

        if DUMP:
            emit_dumps()

        snapf = dict(S.cnt)

        def fin(gp, snapf=snapf):
            for kk in tuple(S.dsems):
                if snapf.get(kk, 0):
                    gp.wait_ge(S.sems[kk], snapf.get(kk, 0))
            return gp.engine_nop()
        A("GP", fin, wait=(f"outst{NCH-1}",), tag="fin")

        S.emit(E(nc.Block()))
    return nc


def shard_inputs(inputs, core):
    m = dict(
        x=np.ascontiguousarray(np.asarray(inputs["L0_attn_x"])[core]),
        wqT=np.ascontiguousarray(np.asarray(inputs["L0_attn_W_q"]).T),
        wkT=np.ascontiguousarray(np.asarray(inputs["L0_attn_W_k"]).T),
        wvT=np.ascontiguousarray(np.asarray(inputs["L0_attn_W_v"]).T),
        woT=np.ascontiguousarray(np.asarray(inputs["L0_attn_W_o"]).T),
        gnw=np.ascontiguousarray(np.asarray(inputs["L0_attn_gn_weight"])),
        gnb=np.ascontiguousarray(np.asarray(inputs["L0_attn_gn_bias"])),
        clipx=np.asarray(inputs["L0_ffn_clip_x"], np.float32).reshape(1, 1),
        clipd=np.asarray(inputs["L0_ffn_clip_down"], np.float32).reshape(1, 1),
        wg=np.ascontiguousarray(np.asarray(inputs["L0_ffn_w_gate"])),
        wu=np.ascontiguousarray(np.asarray(inputs["L0_ffn_w_up"])),
        wd=np.ascontiguousarray(np.asarray(inputs["L0_ffn_w_down"])),
        wgT=np.ascontiguousarray(np.asarray(inputs["L0_ffn_w_gate"]).T),
        wuT=np.ascontiguousarray(np.asarray(inputs["L0_ffn_w_up"]).T),
        wdT=np.ascontiguousarray(np.asarray(inputs["L0_ffn_w_down"]).T),
        cvec=np.ascontiguousarray(np.asarray(inputs["L0_c"])[core:core + 1]),
        wmod=np.ascontiguousarray(np.asarray(inputs["L0_wmod"])),
    )
    m.update(host_constants())
    return {k: np.asarray(v) for k, v in m.items()}


def kernel(**inputs):
    T = np.asarray(inputs["L0_attn_x"]).shape[1]
    nc = build_program(T)
    in_maps = [shard_inputs(inputs, c) for c in range(NCORE)]
    res = run_bass_kernel_spmd(nc, in_maps, core_ids=list(range(NCORE)))
    return np.stack([np.asarray(r["out"], np.float32) for r in res.results], axis=0)
